# version 30
# baseline (speedup 1.0000x reference)
import numpy as np
import ml_dtypes
import concourse.bass as bass
import concourse.mybir as mybir
from concourse.bass_utils import run_bass_kernel_spmd

F32 = mybir.dt.float32
BF16 = mybir.dt.bfloat16
AF = mybir.ActivationFunctionType
ALU = mybir.AluOpType
BF = ml_dtypes.bfloat16
NEG = -3.0e38

D = 1024
HEADS = 8
DH = 64
WIN = 512
HID = 2730
HIDP = 2816
KT = HIDP // 128


def _tile_rows(w):
    K = w.shape[0] // 128
    return np.ascontiguousarray(w.reshape(K, 128, -1).transpose(1, 0, 2))

def host_tables(T):
    t = {}
    inv = 1.0 / (10000.0 ** (np.arange(0, D, 2, dtype=np.float64) / D))
    ang = np.arange(T, dtype=np.float64)[None, :] * inv[:, None]
    t["ropeC"] = _tile_rows(np.cos(ang).astype(np.float32))
    t["ropeS"] = _tile_rows(np.sin(ang).astype(np.float32))
    half = np.arange(0, DH, 2, dtype=np.float64)
    invf = 1.0 / 10000.0 ** (half / DH)
    tt = np.arange(T, dtype=np.float64)
    fr = tt[:, None] * invf[None, :]
    freqs = np.concatenate([fr, fr], axis=1)
    base = (half + 0.4 * DH) / (1.4 * DH)
    power = (tt - T // 2) / float(WIN // 2)
    sc = base[None, :] ** power[:, None]
    scale = np.concatenate([sc, sc], axis=1)
    sign = np.where(np.arange(DH) < 32, -1.0, 1.0)
    cos, sin = np.cos(freqs), np.sin(freqs)
    rep = lambda a: np.ascontiguousarray(np.tile(a.T, (2, 1)).astype(np.float32))
    t["xq1"] = rep(cos * scale)
    t["xq2"] = rep(sign[None, :] * sin * scale)
    t["xk1"] = rep(cos / scale)
    t["xk2"] = rep(sign[None, :] * sin / scale)
    t["identbf"] = np.eye(128, dtype=np.float32).astype(BF)
    sw = np.zeros((128, 128), np.float32)
    for p in range(128):
        sw[p, p ^ 32] = 1.0
    t["swap32"] = sw.astype(BF)
    t["identf"] = np.eye(128, dtype=np.float32)
    t["onesbf"] = np.ones((128, 1), dtype=np.float32).astype(BF)
    t["onesfrow"] = np.ones((1, 128), dtype=np.float32)
    t["onesbrow"] = np.ones((1, 128), dtype=ml_dtypes.bfloat16)
    e0 = np.zeros((1, 128), np.float32); e0[0, :64] = 1.0
    e1 = np.zeros((1, 128), np.float32); e1[0, 64:] = 1.0
    t["e0row"], t["e1row"] = e0, e1
    es = np.zeros((128, 4, 8), np.float32)
    for tile in range(4):
        for p in range(128):
            es[p, tile, 2 * tile + p // 64] = 1.0
    t["Esel"] = es.astype(BF)
    bs = np.zeros((8, 4, 128), np.float32)
    for tile in range(4):
        for p in range(128):
            bs[2 * tile + p // 64, tile, p] = 1.0
    t["Bsel"] = np.ascontiguousarray(bs.reshape(8, 512)).astype(BF)
    r = np.arange(128)[:, None]
    cc = np.arange(1024)[None, :]
    madd = np.zeros((128, 8, 1024), np.float32)
    for itp in range(4):
        st = r + 128 * itp
        keep = (cc >= st) & (cc < st + 513)
        madd[:, itp, :] = np.where(keep, 0.0, -1e30)
        keep0 = keep & (cc >= 512)
        madd[:, 4 + itp, :] = np.where(keep0, 0.0, -1e30)
    t["maskAdd"] = madd.astype(BF)
    r1 = np.arange(128)[:, None]
    c1 = np.arange(128)[None, :]
    me = np.zeros((128, 2, 128), np.float32)
    me[:, 0, :] = np.where(c1 >= r1, 0.0, -1e30)
    me[:, 1, :] = np.where(c1 <= r1, 0.0, -1e30)
    t["maskEdge"] = me.astype(BF)
    return t


def host_prep(inputs, T):
    g = lambda k: np.asarray(inputs[k])
    m = dict(host_tables(T))
    perm = np.concatenate([np.arange(0, D, 2), np.arange(1, D, 2)])
    f32 = np.float32
    m["wqT"] = _tile_rows(np.ascontiguousarray(g("L0_attn_wq_w").T[:, perm])).astype(f32)
    m["wkT"] = _tile_rows(np.ascontiguousarray(g("L0_attn_wk_w").T[:, perm])).astype(f32)
    m["wvT"] = _tile_rows(np.ascontiguousarray(g("L0_attn_wv_w").T)).astype(f32)
    m["wqB"] = np.ascontiguousarray(g("L0_attn_wq_b")[perm].reshape(8, 128).T).astype(f32)
    m["wkB"] = np.ascontiguousarray(g("L0_attn_wk_b")[perm].reshape(8, 128).T).astype(f32)
    m["wvB"] = np.ascontiguousarray(g("L0_attn_wv_b").reshape(8, 128).T).astype(f32)
    wqkv = g("L0_ffn_wqkv"); wo = g("L0_ffn_wo")
    qsc = g("L0_ffn_q_scale"); ksc = g("L0_ffn_k_scale")
    lg = g("L0_ffn_ln_g"); lb = g("L0_ffn_ln_b")
    w1 = g("L0_ffn_w1"); b1 = g("L0_ffn_b1")
    w2 = g("L0_ffn_w2"); b2 = g("L0_ffn_b2")
    for l in range(2):
        m[f"wqkv{l}"] = _tile_rows(wqkv[l]).astype(f32)
        m[f"wo{l}"] = _tile_rows(wo[l]).astype(f32)
        m[f"qsc{l}"] = np.tile(qsc[l], 2).reshape(128, 1).astype(f32)
        m[f"ksc{l}"] = np.tile(ksc[l], 2).reshape(128, 1).astype(f32)
        swp = np.arange(64) ^ 32
        m[f"qscS{l}"] = np.tile(qsc[l][swp], 2).reshape(128, 1).astype(f32)
        m[f"kscS{l}"] = np.tile(ksc[l][swp], 2).reshape(128, 1).astype(f32)
        m[f"lnG{l}"] = np.ascontiguousarray(lg[l].reshape(8, 128).T).astype(f32)
        m[f"lnB{l}"] = np.ascontiguousarray(lb[l].reshape(8, 128).T).astype(f32)
        w1a = np.zeros((D, HIDP), f32); w1a[:, :HID] = w1[l][:, :HID]
        w1g = np.zeros((D, HIDP), f32); w1g[:, :HID] = w1[l][:, HID:]
        m[f"w1a{l}"] = np.ascontiguousarray(
            _tile_rows(w1a).reshape(128, 8, KT, 128).transpose(2, 0, 1, 3))
        m[f"w1g{l}"] = np.ascontiguousarray(
            _tile_rows(w1g).reshape(128, 8, KT, 128).transpose(2, 0, 1, 3))
        b1a = np.zeros(HIDP, f32); b1a[:HID] = b1[l][:HID]
        b1g = np.zeros(HIDP, f32); b1g[:HID] = b1[l][HID:]
        m[f"b1a{l}"] = np.ascontiguousarray(b1a.reshape(KT, 128).T)
        m[f"b1g{l}"] = np.ascontiguousarray(b1g.reshape(KT, 128).T)
        w2p = np.zeros((HIDP, D), f32); w2p[:HID] = w2[l]
        m[f"w2p{l}"] = np.ascontiguousarray(_tile_rows(w2p).transpose(1, 0, 2))
        m[f"b2c{l}"] = np.ascontiguousarray(b2[l].reshape(8, 128).T).astype(f32)
    return m


def prep_x(xb):
    return np.ascontiguousarray(xb.T.reshape(8, 128, -1).transpose(1, 0, 2))

def unprep_out(o):
    return np.ascontiguousarray(o.transpose(1, 0, 2).reshape(1024, -1).T)



STREAM_OF = {"pe": "P", "dve": "V", "sc": "S", "gp": "G",
             "qsC": "Q", "qsL": "Q", "qsL2": "Q", "qsS": "Q",
             "qgW": "G", "qgW2": "G", "qgS": "G"}
SELF_WAIT = ("pe", "dve", "sc", "gp")

class KB:
    def __init__(self, T):
        self.T = T
        self.nc = bass.Bass("TRN2", target_bir_lowering=False, debug=False)
        n = self.nc
        self.e = {"pe": n.tensor, "dve": n.vector, "sc": n.scalar,
                  "gp": n.gpsimd, "qsC": n.sync, "qsL": n.sync,
                  "qsL2": n.sync, "qsS": n.sync, "qgW": n.gpsimd,
                  "qgW2": n.gpsimd, "qgS": n.gpsimd}
        self.sem = {k: n.alloc_semaphore("sem_" + k) for k in self.e}
        self.cnt = {k: 0 for k in self.e}
        self.step = {"pe": 1, "dve": 1, "sc": 1, "gp": 1,
                     "qsC": 16, "qsL": 16, "qsL2": 16, "qsS": 16,
                     "qgW": 16, "qgW2": 16, "qgS": 16}
        self.lastw = {s: {k: 0 for k in self.e} for s in "PVSGQ"}

    def wait(self, k, deps):
        st = STREAM_OF[k]
        for (j, v) in deps:
            if v is None or v <= 0:
                continue
            if self.lastw[st][j] >= v:
                continue
            self.e[k].wait_ge(self.sem[j], v)
            self.lastw[st][j] = v

    def op(self, k, name, *a, deps=(), inc=True, **kw):
        deps = list(deps)
        if k in SELF_WAIT:
            deps.append((k, self.cnt[k]))
        self.wait(k, deps)
        ins = getattr(self.e[k], name)(*a, **kw)
        if inc:
            self.cnt[k] += self.step[k]
            ins.then_inc(self.sem[k], self.step[k])
            return self.cnt[k]
        return None

    def dma(self, k, out, in_, deps=(), **kw):
        c = self.op(k, "dma_start", out, in_, deps=deps, **kw)
        return (k, c)

    def dmaL(self, i, out, in_, deps=()):
        k = "qsL" if i % 2 == 0 else "qsL2"
        return self.dma(k, out, in_, deps=list(deps) + [(k, self.cnt[k])])

    def dmaW(self, i, out, in_, deps=()):
        k = "qgW" if i % 2 == 0 else "qgW2"
        return self.dma(k, out, in_, deps=list(deps) + [(k, self.cnt[k])])

    def mmacc(self, out, pairs, deps=(), tile_position=None, transpose=False):
        deps = list(deps) + [("pe", self.cnt["pe"])]
        self.wait("pe", deps)
        n = len(pairs)
        ins = None
        for i, (l, r) in enumerate(pairs):
            kw = dict(start=(i == 0), stop=(i == n - 1))
            if tile_position is not None:
                kw["tile_position"] = tile_position
            if transpose:
                kw["is_transpose"] = True
            ins = self.e["pe"].matmul(out, l, r, **kw)
        self.cnt["pe"] += 1
        ins.then_inc(self.sem["pe"], 1)
        return self.cnt["pe"]

    def trpack(self, outs_ins, ident, deps=()):
        deps = list(deps) + [("pe", self.cnt["pe"])]
        self.wait("pe", deps)
        n = len(outs_ins)
        ins = None
        for i, (o, s) in enumerate(outs_ins):
            ins = self.e["pe"].matmul(o, s, ident, is_transpose=True,
                                      start=(i == 0), stop=(i == n - 1))
        self.cnt["pe"] += 1
        ins.then_inc(self.sem["pe"], 1)
        return self.cnt["pe"]

    def qdep(self, ch):
        return (ch, self.cnt[ch])

    def barrier(self):
        for q in ("qsC", "qsL", "qsL2", "qsS", "qgW", "qgW2", "qgS"):
            self.wait(q, [(q, self.cnt[q])])
        self.nc.all_engine_barrier()
        for s in "PVSGQ":
            for k in self.e:
                self.lastw[s][k] = max(self.lastw[s][k], self.cnt[k])

    def allc(self, k):
        return (k, self.cnt[k])



def build_nc(T=4096, dbg=False, nlayers=2):
    kb = KB(T)
    K = kb
    nc = kb.nc
    NCH = T // 512
    NSUP = T // 512
    NT = T // 128
    NC6 = max(1, T // 1024)
    CH6 = T // NC6

    di = lambda nm, sh, dt=F32: nc.dram_tensor(nm, sh, dt, kind="ExternalInput").ap()
    dkind = "ExternalOutput" if dbg else "Internal"
    xT = di("xT", [128, 8, T])
    OUT = nc.dram_tensor("OUT", [128, 8, T], F32, kind="ExternalOutput").ap()
    wqT = di("wqT", [128, 8, 1024]); wkT = di("wkT", [128, 8, 1024])
    wvT = di("wvT", [128, 8, 1024])
    wqB = di("wqB", [128, 8]); wkB = di("wkB", [128, 8]); wvB = di("wvB", [128, 8])
    ropeC = di("ropeC", [128, 4, T]); ropeS = di("ropeS", [128, 4, T])
    XQ1 = di("xq1", [128, T]); XQ2 = di("xq2", [128, T])
    XK1 = di("xk1", [128, T]); XK2 = di("xk2", [128, T])
    identbf = di("identbf", [128, 128], BF16)
    swap32 = di("swap32", [128, 128], BF16)
    identf = di("identf", [128, 128])
    onesbf = di("onesbf", [128, 1], BF16)
    onesfrow = di("onesfrow", [1, 128])
    onesbrow = di("onesbrow", [1, 128], BF16)
    e0row = di("e0row", [1, 128]); e1row = di("e1row", [1, 128])
    Esel = di("Esel", [128, 4, 8], BF16)
    Bsel = di("Bsel", [8, 512], BF16)
    maskAdd = di("maskAdd", [128, 8, 1024], BF16)
    maskEdge = di("maskEdge", [128, 2, 128], BF16)
    WQKV = [di(f"wqkv{l}", [128, 8, 1536]) for l in range(2)]
    WO = [di(f"wo{l}", [128, 4, 1024]) for l in range(2)]
    QSC = [di(f"qsc{l}", [128, 1]) for l in range(2)]
    KSC = [di(f"ksc{l}", [128, 1]) for l in range(2)]
    QSCS = [di(f"qscS{l}", [128, 1]) for l in range(2)]
    KSCS = [di(f"kscS{l}", [128, 1]) for l in range(2)]
    LNG = [di(f"lnG{l}", [128, 8]) for l in range(2)]
    LNB = [di(f"lnB{l}", [128, 8]) for l in range(2)]
    W1A = [di(f"w1a{l}", [KT, 128, 8, 128]) for l in range(2)]
    W1G = [di(f"w1g{l}", [KT, 128, 8, 128]) for l in range(2)]
    B1A = [di(f"b1a{l}", [128, KT]) for l in range(2)]
    B1G = [di(f"b1g{l}", [128, KT]) for l in range(2)]
    W2P = [di(f"w2p{l}", [KT, 128, 1024]) for l in range(2)]
    B2C = [di(f"b2c{l}", [128, 8]) for l in range(2)]

    dsc = lambda nm, sh, dt: nc.dram_tensor(nm, sh, dt, kind=dkind).ap()
    hX1 = dsc("hX1", [128, 8, T], F32)
    hY = dsc("hY", [128, 8, T], F32)
    hQh = dsc("hQh", [128, 8, T], BF16)
    hKh = dsc("hKh", [128, 8, T], BF16)
    hQi = dsc("hQi", [128, 4, T], BF16)
    hOi = dsc("hOi", [128, 4, T], BF16)

    PS0 = nc.alloc_psum_tensor("PS0", [128, 1024], F32).ap()
    PS1 = nc.alloc_psum_tensor("PS1", [128, 1024], F32).ap()
    PS2 = nc.alloc_psum_tensor("PS2", [128, 512], F32).ap()
    PS3 = nc.alloc_psum_tensor("PS3", [128, 512], F32).ap()
    PTR0 = nc.alloc_psum_tensor("PTR0", [128, 1024], BF16).ap()
    PTR1 = nc.alloc_psum_tensor("PTR1", [128, 1024], BF16).ap()
    PTR0f = PTR0.bitcast(F32)
    PTR1f = PTR1.bitcast(F32)
    PS0a, PS0b = PS0[:, 0:512], PS0[:, 512:1024]
    PS1a, PS1b = PS1[:, 0:512], PS1[:, 512:1024]

    ARENA_SZ = 204 * 1024
    _arena = nc.alloc_sbuf_tensor("arena", [128, ARENA_SZ], mybir.dt.uint8)
    _base = nc.lookup_mloc(_arena).addr
    A = lambda nm, sh, dt, off: nc.alloc_sbuf_tensor_at(nm, sh, dt,
                                                        offset=_base + off)
    co = 0
    def CA(nm, sh, dt):
        nonlocal co
        h = A(nm, sh, dt, co)
        per = int(np.prod(sh[1:])) * (2 if dt == BF16 else 4)
        co += (per + 31) // 32 * 32
        return h
    c_identbf = CA("c_identbf", [128, 128], BF16)
    c_swap32 = CA("c_swap32", [128, 128], BF16)
    c_identf = CA("c_identf", [128, 128], F32)
    c_onesbf = CA("c_onesbf", [128, 1], BF16)
    c_onesfrow = CA("c_onesfrow", [1, 128], F32)
    c_onesbrow = CA("c_onesbrow", [1, 128], BF16)
    c_e0 = CA("c_e0", [1, 128], F32)
    c_e1 = CA("c_e1", [1, 128], F32)
    c_Esel = CA("c_Esel", [128, 4, 8], BF16)
    c_Bsel = CA("c_Bsel", [8, 512], BF16)
    c_wqB = CA("c_wqB", [128, 8], F32)
    c_wkB = CA("c_wkB", [128, 8], F32)
    c_wvB = CA("c_wvB", [128, 8], F32)
    c_qsc = CA("c_qsc", [128, 1], F32)
    c_ksc = CA("c_ksc", [128, 1], F32)
    c_qscS = CA("c_qscS", [128, 1], F32)
    c_kscS = CA("c_kscS", [128, 1], F32)
    c_lnG = CA("c_lnG", [128, 8], F32)
    c_lnB = CA("c_lnB", [128, 8], F32)
    c_b1a = CA("c_b1a", [128, KT], F32)
    c_b1g = CA("c_b1g", [128, KT], F32)
    c_b2 = CA("c_b2", [128, 8], F32)
    c_pm2 = CA("c_pm2", [128, 512], BF16)
    assert co <= 8 * 1024, co

    FB = 8 * 1024
    fo = FB
    def FA(nm, sh, dt=F32):
        nonlocal fo
        h = A(nm, sh, dt, fo)
        per = int(np.prod(sh[1:])) * (2 if dt == BF16 else 4)
        fo += (per + 31) // 32 * 32
        return h
    r_mu = FA("r_mu", [1, 512]); r_m2 = FA("r_m2", [1, 512])
    r_rstd = FA("r_rstd", [1, 512])
    r_t0 = FA("r_t0", [8, 512]); r_t1 = FA("r_t1", [8, 512])
    s_B1 = FA("s_B1", [128, 512]); s_B2 = FA("s_B2", [128, 512])
    s_r8a = FA("s_r8a", [8, 512]); s_r8b = FA("s_r8b", [8, 512])
    s_r8c = FA("s_r8c", [8, 512])
    s_Brinv = FA("s_Brinv", [128, 512])
    s_cols = FA("s_cols", [128, 32])
    assert fo <= 32 * 1024, fo
    r_var = r_m2
    r_nmr = r_mu
    s_rrow = A("s_rrow", [1, 512], F32, 8 * 1024 + 2048 * 3)
    s_rrow2 = A("s_rrow2", [1, 512], F32, 8 * 1024 + 2048 * 4)
    r_hiB = A("r_hiB", [1, 512], BF16, 8 * 1024 + 2048 * 7)
    r_loB = A("r_loB", [1, 512], BF16, 8 * 1024 + 2048 * 7 + 1024)
    r_hiC = A("r_hiC", [1, 512], BF16, 8 * 1024 + 2048 * 8)
    r_loC = A("r_loC", [1, 512], BF16, 8 * 1024 + 2048 * 8 + 1024)

    BIG = 32 * 1024
    TOP = 204 * 1024
    p1_wq = A("p1_wq", [128, 8, 1024], BF16, BIG)
    p1_wk = A("p1_wk", [128, 8, 1024], BF16, BIG + 16384)
    p1_wv = A("p1_wv", [128, 8, 1024], BF16, BIG + 32768)
    VOFF = BIG + 49152
    p1_V = A("p1_V", [128, NT, 1024], BF16, VOFF)
    VSZ = NT * 1024 * 2
    o = VOFF + VSZ
    p1_xbf = [A(f"p1_xbf{i}", [128, 8, 512], BF16, o + i * 8192) for i in range(2)]
    o += 16384
    p1_sq = A("p1_sq", [128, 8, 512], BF16, o); o += 8192
    p1_lnx = [A(f"p1_lnx{i}", [128, 8, 512], BF16, o + i * 8192) for i in range(2)]
    o += 16384
    p1_rc = [A(f"p1_rc{i}", [128, 4, 512], BF16, o + i * 4096) for i in range(2)]
    o += 8192
    p1_rs = [A(f"p1_rs{i}", [128, 4, 512], BF16, o + i * 4096) for i in range(2)]
    o += 8192
    p1_xfd = [A("p1_xfdA", [128, 512], F32, FB + 14336),
              A("p1_xfdB", [128, 512], F32, FB + 16384)]
    p1_qe = A("p1_qe", [128, 2, 512], BF16, o); o += 2048
    p1_qo = A("p1_qo", [128, 2, 512], BF16, o); o += 2048
    p1_m1 = A("p1_m1", [128, 512], BF16, FB + 22656)
    p1_m2 = c_pm2
    p1_tmp = A("p1_tmp", [128, 512], F32, FB + 18432)
    p3_y = [A(f"p3_y{i}", [128, 512], F32, FB + 20480) for i in range(1)]
    assert o <= TOP, o

    p2_PT = A("p2_PT", [128, NT, 512], BF16, BIG)
    assert BIG + NT * 1024 <= VOFF
    p2_kc = [A(f"p2_kc{i}", [128, 8, 512], BF16, VOFF - 16384 + i * 8192)
             for i in range(2)]
    assert NT * 1024 <= 49152 - 16384
    o2 = VOFF + VSZ
    p2_S = [A(f"p2_S{i}", [128, T], BF16, o2 + i * T * 2) for i in range(4)]
    o2 += 4 * T * 2
    p2_P = [A(f"p2_P{i}", [128, T], BF16, o2 + i * T * 2) for i in range(2)]
    o2 += 2 * T * 2
    p2_qsl = A("p2_qsl", [128, 8, 512], BF16, o2); o2 += 8192
    p2_mp = A("p2_mp", [128, 4, 16], F32, o2); o2 += 256
    assert o2 <= TOP, o2
    p2_xs = [A("p2_xsA", [128, 512], F32, FB), A("p2_xsB", [128, 512], F32, FB + 2048)]
    p2_x1 = [A("p2_x1A", [128, 512], F32, FB + 4096),
             A("p2_x1B", [128, 512], F32, FB + 8192)]
    p2_tmp = A("p2_tmpX", [128, 512], F32, FB + 10240)

    o4 = BIG
    p4_Kp = A("p4_Kp", [128, 4, WIN + T], BF16, o4); o4 += 4 * (WIN + T) * 2
    p4_Vp = A("p4_Vp", [128, NT + 4, 512], BF16, o4); o4 += (NT + 4) * 1024
    p4_wo = A("p4_wo", [128, 4, 1024], BF16, o4); o4 += 8192
    moff = o4
    p4_msk = A("p4_msk", [128, 8, 1024], BF16, o4); o4 += 16384
    p4_me = A("p4_me", [128, 2, 128], BF16, o4); o4 += 512
    O4B = o4
    oq = O4B
    p4_wqkv = A("p4_wqkv", [128, 8, 1536], BF16, oq); oq += 24576
    p4_hs = [A(f"p4_hs{i}", [128, 8, 512], BF16, oq + i * 8192) for i in range(2)]
    oq += 16384
    p4_qc = [A(f"p4_qc{i}", [128, 4, 512], BF16, oq + i * 4096) for i in range(2)]
    oq += 8192
    p4_kc2 = [A(f"p4_kc2{i}", [128, 4, 512], BF16, oq + i * 4096) for i in range(2)]
    oq += 8192
    p4_sq = A("p4_sq", [128, 4, 512], BF16, oq); oq += 4096
    p4_tb = A("p4_tb", [128, 4, 512], BF16, oq); oq += 4096
    p4_tf = [A(f"p4_tf{i}", [128, 512], F32, oq + i * 2048) for i in range(2)]
    oq += 4096
    p4_brk = A("p4_brk", [128, 512], BF16, oq); oq += 1024
    p4_qh = A("p4_qh", [128, 512], BF16, oq); oq += 1024
    p4_m1 = A("p4_m1", [128, 512], BF16, oq); oq += 1024
    p4_m2 = A("p4_m2", [128, 512], BF16, oq); oq += 1024
    p4_rkb = A("p4_rkb", [8, 512], BF16, oq); oq += 1024
    p4_rqb = A("p4_rqb", [8, 512], BF16, oq); oq += 1024
    assert oq <= TOP, oq
    oa = O4B
    p4_PT = [A(f"p4_PT{i}", [128, 8, 512], BF16, oa + i * 8192) for i in range(2)]
    oa += 16384
    p4_P2 = [A(f"p4_P2{i}", [128, 1024], BF16, oa + i * 2048) for i in range(2)]
    oa += 4096
    p4_P2 += [A(f"p4_P2{i + 2}", [128, 1024], BF16, moff + i * 2048)
              for i in range(2)]
    p4_qi = [A(f"p4_qi{i}", [128, 4, 512], BF16, oa + i * 4096) for i in range(2)]
    oa += 8192
    p4_Og = [A(f"p4_Og{i}", [128, 4, 512], BF16, oa + i * 4096) for i in range(2)]
    oa += 8192
    p4_ysf = [A(f"p4_ysf{i}", [128, 8, 512], F32, oa + i * 16384) for i in range(2)]
    oa += 32768
    p4_y1 = [A(f"p4_y1N{i}", [128, 512], F32, oa + i * 2048) for i in range(2)]
    oa += 4096
    p4_mxb = A("p4_mxb", [128, 8], F32, oa); oa += 32
    p4_rsb = A("p4_rsb", [128, 8], F32, oa); oa += 32
    p4_rsi = A("p4_rsi", [128, 8], F32, oa); oa += 32
    p4_nb = A("p4_nb", [128, 8], F32, oa); oa += 32
    assert oa <= TOP

    o6 = BIG
    p6_At = [A(f"p6_At{i}", [128, 512], BF16, o6 + i * 1024) for i in range(2)]
    o6 += 2048
    p6_G = A("p6_G", [128, KT, CH6], BF16, o6); o6 += KT * CH6 * 2
    p6_W2 = A("p6_W2", [128, KT, 1024], BF16, o6); o6 += KT * 2048
    p6_hb = A("p6_hb", [128, 8, CH6], BF16, o6); o6 += 8 * CH6 * 2
    p6_ln = A("p6_ln", [128, 8, CH6], BF16, o6); o6 += 8 * CH6 * 2
    p6_sq = A("p6_sq", [128, 8, 512], BF16, o6); o6 += 8192
    p6_hf = [A(f"p6_hf{i}", [128, 512], F32, o6 + i * 2048) for i in range(2)]
    o6 += 4096
    p6_w1a = [A(f"p6_w1a{i}", [128, 8, 128], BF16, o6 + i * 2048) for i in range(2)]
    o6 += 4096
    p6_w1g = [A(f"p6_w1g{i}", [128, 8, 128], BF16, o6 + i * 2048) for i in range(2)]
    o6 += 4096
    p6_ys = [A(f"p6_ys{i}", [128, 512], F32, o6 + i * 2048) for i in range(2)]
    o6 += 4096
    p6_xs = [A(f"p6_xs{i}", [128, 512], F32, o6 + i * 2048) for i in range(2)]
    o6 += 4096
    p6_t1 = A("p6_t1", [128, 512], F32, o6); o6 += 2048
    p6_bnd = A("p6_bnd", [128, 4, 8], F32, o6); o6 += 128
    p6_ln2 = A("p6_ln2", [128, 8, CH6], BF16, o6); o6 += 8 * CH6 * 2
    p6_lnd = [p6_ln, p6_ln2]
    assert o6 <= TOP, o6

    for dst, src in ((c_identbf, identbf), (c_swap32, swap32),
                     (c_identf, identf), (c_onesbf, onesbf),
                     (c_onesfrow, onesfrow), (c_onesbrow, onesbrow),
                     (c_e0, e0row), (c_e1, e1row),
                     (c_Esel, Esel), (c_Bsel, Bsel), (c_wqB, wqB),
                     (c_wkB, wkB), (c_wvB, wvB)):
        K.dma("qsC", dst.ap(), src)
    for dst, src in ((p1_wq, wqT), (p1_wk, wkT), (p1_wv, wvT)):
        K.dma("qgS", dst.ap(), src)
    consts_done = K.cnt["qsC"]
    dep_w = ("qgS", K.cnt["qgS"])

    ln_evac = [0]

    def ln_stats(xbf_ap, sq_ap, dep_load, nfeat=8, csq=None, bdst=None,
                 bextra=()):
        if csq is None:
            csq = K.op("sc", "activation", sq_ap, xbf_ap, AF.Square,
                       deps=[dep_load, ("pe", K.cnt["pe"])])
        dep = [("sc", csq), ("qsC", K.cnt["qsC"]), ("dve", ln_evac[0])]
        if dep_load is not None:
            dep.append(dep_load)
        c1 = K.mmacc(PS2[0:1, :], [(c_onesbf.ap(), xbf_ap[:, dt, :])
                                   for dt in range(nfeat)], deps=dep)
        c2 = K.mmacc(PS3[0:1, :], [(c_onesbf.ap(), sq_ap[:, dt, :])
                                   for dt in range(nfeat)], deps=dep)
        nf = float(nfeat * 128)
        K.op("dve", "tensor_scalar_mul", r_mu.ap(), PS2[0:1, :], 1.0 / nf,
             deps=[("pe", c1)])
        ev2 = K.op("dve", "tensor_scalar_mul", r_m2.ap(), PS3[0:1, :], 1.0 / nf,
                   deps=[("pe", c2)])
        t0 = r_t0.ap()[0:1, :]
        t1 = r_t1.ap()[0:1, :]
        K.op("dve", "tensor_mul", t0, r_mu.ap(), r_mu.ap())
        K.op("dve", "tensor_sub", r_var.ap(), r_m2.ap(), t0)
        cv = K.op("dve", "tensor_scalar_add", r_var.ap(), r_var.ap(), 1e-5)
        cs = K.op("sc", "activation", t0, r_var.ap(), AF.Sqrt,
                  deps=[("dve", cv)])
        K.op("dve", "reciprocal", r_rstd.ap(), t0, deps=[("sc", cs)])
        K.op("dve", "tensor_mul", t0, r_rstd.ap(), r_rstd.ap())
        K.op("dve", "tensor_mul", t1, r_var.ap(), t0)
        K.op("dve", "tensor_scalar", t1, t1, -0.5, 1.5,
             ALU.mult, ALU.add)
        K.op("dve", "tensor_mul", r_rstd.ap(), r_rstd.ap(), t1)
        K.op("dve", "tensor_mul", r_nmr.ap(), r_mu.ap(), r_rstd.ap())
        K.op("dve", "tensor_scalar_mul", r_nmr.ap(), r_nmr.ap(), -1.0)
        K.op("dve", "tensor_copy", r_hiB.ap(), r_rstd.ap())
        K.op("dve", "tensor_sub", t0, r_rstd.ap(), r_hiB.ap())
        cl1 = K.op("dve", "tensor_copy", r_loB.ap(), t0)
        K.op("dve", "tensor_copy", r_hiC.ap(), r_nmr.ap())
        K.op("dve", "tensor_sub", t1, r_nmr.ap(), r_hiC.ap())
        cl2 = K.op("dve", "tensor_copy", r_loC.ap(), t1)
        if bdst is not None:
            cb1 = K.mmacc(bdst[0], [(c_onesbrow.ap(), r_hiB.ap()),
                                    (c_onesbrow.ap(), r_loB.ap())],
                          deps=[("dve", cl2), ("qsC", K.cnt["qsC"])]
                               + list(bextra))
            cb2 = K.mmacc(bdst[1], [(c_onesbrow.ap(), r_hiC.ap()),
                                    (c_onesbrow.ap(), r_loC.ap())],
                          deps=[])
            ln_evac[0] = ev2
            return ("pe", cb2)
        cb1 = K.mmacc(PS2, [(c_onesbrow.ap(), r_hiB.ap()),
                            (c_onesbrow.ap(), r_loB.ap())],
                      deps=[("dve", cl1), ("qsC", K.cnt["qsC"])])
        cb2 = K.mmacc(PS3, [(c_onesbrow.ap(), r_hiC.ap()),
                            (c_onesbrow.ap(), r_loC.ap())],
                      deps=[("dve", cl2)])
        K.op("dve", "tensor_copy", s_B1.ap(), PS2, deps=[("pe", cb1)])
        cvf = K.op("dve", "tensor_copy", s_B2.ap(), PS3, deps=[("pe", cb2)])
        ln_evac[0] = cvf
        return ("dve", cvf)

    ev4 = [0, 0, 0, 0]
    qe_free = [0]
    stat_pe = [0]
    sq_hold = {}
    rope_done = [(0, 0), (0, 0)]

    def p01_part1(c):
        c0 = c * 512
        hb = c % 2
        lb_ = K.dmaW(c, p1_xbf[hb].ap(), xT[:, :, c0:c0 + 512],
                     deps=[("sc", K.cnt["sc"]), ("pe", stat_pe[0])])
        K.dmaW(c, p1_rc[hb].ap(), ropeC[:, :, c0:c0 + 512],
               deps=[("dve", rope_done[hb][0])])
        K.dmaW(c, p1_rs[hb].ap(), ropeS[:, :, c0:c0 + 512])
        sq_hold[c] = K.op("sc", "activation", p1_sq.ap(), p1_xbf[hb].ap(),
                          AF.Square, deps=[lb_, ("pe", stat_pe[0])])

    def p01_part2(c):
        c0 = c * 512
        hb = c % 2
        cb = ln_stats(p1_xbf[hb].ap(), p1_sq.ap(), None, csq=sq_hold[c])
        stat_pe[0] = K.cnt["pe"]
        assert cb[0] == "dve"
        lnx = p1_lnx[hb]
        cl = None
        for dt in range(8):
            lf = K.dmaL(dt, p1_xfd[dt % 2].ap(), xT[:, dt, c0:c0 + 512],
                        deps=[("dve", K.cnt["dve"])])
            K.op("dve", "tensor_mul", p1_tmp.ap(), p1_xfd[dt % 2].ap(),
                 s_B1.ap(), deps=[lf, cb])
            cl = K.op("dve", "tensor_add", lnx.ap()[:, dt, :], p1_tmp.ap(),
                      s_B2.ap())
        return ("dve", cl)

    def p01_s2(c, dep_lnx):
        c0 = c * 512
        hb = c % 2
        lnx = p1_lnx[hb]
        bank = [PS0a, PS0b, PS1a, PS1b]
        bi = 0
        for tt2 in range(4):
            for eh in range(2):
                ps = bank[bi % 4]
                cm = K.mmacc(ps, [(lnx.ap()[:, dt, tt2 * 128:(tt2 + 1) * 128],
                                   p1_wv.ap()[:, dt, eh * 512:(eh + 1) * 512])
                                  for dt in range(8)],
                             deps=[dep_lnx, dep_w, ("sc", ev4[bi % 4])])
                ev4[bi % 4] = K.op("sc", "activation",
                                   p1_V.ap()[:, c * 4 + tt2, eh * 512:(eh + 1) * 512],
                                   ps, AF.Copy, deps=[("pe", cm)])
                bi += 1
        cv = 0
        for (wsb, bias, hdst) in ((p1_wq, c_wqB, hQh), (p1_wk, c_wkB, hKh)):
            for pair in range(4):
                blo, bhi = bi % 4, (bi + 1) % 4
                clo = K.mmacc(bank[blo], [(wsb.ap()[:, dt, pair * 128:(pair + 1) * 128],
                                           lnx.ap()[:, dt, :]) for dt in range(8)],
                              deps=[dep_lnx, dep_w, ("sc", ev4[blo])])
                chi = K.mmacc(bank[bhi], [(wsb.ap()[:, dt, (pair + 4) * 128:(pair + 5) * 128],
                                           lnx.ap()[:, dt, :]) for dt in range(8)],
                              deps=[("sc", ev4[bhi])])
                ce1 = K.op("sc", "activation", p1_qe.ap()[:, 0, :], bank[blo],
                           AF.Identity, bias=bias.ap()[:, pair:pair + 1],
                           deps=[("pe", clo), ("qsC", consts_done),
                                 ("dve", qe_free[0])])
                ev4[blo] = ce1
                ce2 = K.op("sc", "activation", p1_qe.ap()[:, 1, :], bank[bhi],
                           AF.Identity, bias=bias.ap()[:, pair + 4:pair + 5],
                           deps=[("pe", chi)])
                ev4[bhi] = ce2
                bi += 2
                ct = p1_rc[hb].ap()[:, pair, :]
                st = p1_rs[hb].ap()[:, pair, :]
                K.op("dve", "tensor_mul", p1_m1.ap(), p1_qe.ap()[:, 0, :], ct,
                     deps=[("sc", ce2), ("qgW", K.cnt["qgW"]),
                           ("qgW2", K.cnt["qgW2"]), ("qsS", K.cnt["qsS"])])
                K.op("dve", "tensor_mul", p1_m2.ap(), p1_qe.ap()[:, 1, :], st)
                K.op("dve", "tensor_sub", p1_qo.ap()[:, 0, :], p1_m1.ap(), p1_m2.ap(),
                     deps=[("qsS", K.cnt["qsS"])])
                K.op("dve", "tensor_mul", p1_m1.ap(), p1_qe.ap()[:, 0, :], st)
                qe_free[0] = K.op("dve", "tensor_mul", p1_m2.ap(),
                                  p1_qe.ap()[:, 1, :], ct)
                cv = K.op("dve", "tensor_add", p1_qo.ap()[:, 1, :], p1_m1.ap(),
                          p1_m2.ap())
                K.dma("qsS", hdst[:, pair, c0:c0 + 512], p1_qo.ap()[:, 0, :],
                      deps=[("dve", cv)])
                K.dma("qsS", hdst[:, pair + 4, c0:c0 + 512], p1_qo.ap()[:, 1, :],
                      deps=[("dve", cv)])
        rope_done[hb] = (cv, K.cnt["qsS"])

    p01_part1(0)
    dep_lnx = p01_part2(0)
    p01_part1(1)
    for c in range(NCH):
        dep_nxt = p01_part2(c + 1) if c + 1 < NCH else None
        p01_s2(c, dep_lnx)
        if c + 2 < NCH:
            p01_part1(c + 2)
        dep_lnx = dep_nxt
    K.barrier()

    inv_sd = 1.0 / 32.0
    ev_dep = [0, 0, 0, 0]
    mx_dep = [0, 0, 0, 0]
    exp_cnt2 = [0, 0, 0, 0]
    p_free2 = [0, 0]
    pv_done = K.cnt["pe"]
    ptr_ev2 = [(0, 0), (0, 0)]
    for s in range(NSUP):
        s0 = s * 512
        lq = K.dmaL(s, p2_qsl.ap(), hQh[:, :, s0:s0 + 512],
                    deps=[("pe", K.cnt["pe"])])
        for jj in range(NSUP):
            jb = jj % 2
            lk = K.dmaL(jj, p2_kc[jb].ap(), hKh[:, :, jj * 512:jj * 512 + 512],
                        deps=[("pe", K.cnt["pe"])])
            for itl in range(4):
                ps = (PS0a, PS0b, PS1a, PS1b)[itl]
                cm = K.mmacc(ps, [(p2_qsl.ap()[:, dt, itl * 128:(itl + 1) * 128],
                                   p2_kc[jb].ap()[:, dt, :]) for dt in range(8)],
                             deps=[lk, lq, ("sc", ev_dep[itl]),
                                   ("pe", pv_done)])
                ev_dep[itl] = K.op("sc", "activation",
                                   p2_S[itl].ap()[:, jj * 512:jj * 512 + 512],
                                   ps, AF.Copy, deps=[("pe", cm)])
                mx_dep[itl] = K.op("dve", "reduce_max",
                                   p2_mp.ap()[:, itl, jj:jj + 1],
                                   p2_S[itl].ap()[:, jj * 512:jj * 512 + 512],
                                   axis=mybir.AxisListType.X,
                                   deps=[("sc", ev_dep[itl])])

        def p2_tail(info):
            itl, pb, ce = info
            for w in range((NT + 7) // 8):
                nu = min(8, NT - w * 8)
                tp_ = w % 2
                ptr = PTR0 if tp_ == 0 else PTR1
                ctr = K.trpack(
                    [(ptr[:, u * 128:(u + 1) * 128],
                      p2_P[pb].ap()[:, (w * 8 + u) * 128:(w * 8 + u + 1) * 128])
                     for u in range(nu)], c_identbf.ap(),
                    deps=[("sc", ce), ("dve", ptr_ev2[tp_][0]),
                          ("sc", ptr_ev2[tp_][1]), ("pe", pv_done)])
                p_free2[pb] = ctr
                dst = p2_PT.ap()[:, w * 8:w * 8 + nu, itl * 128:(itl + 1) * 128]
                srcv = ptr[:, 0:nu * 128].rearrange("p (u n) -> p u n", u=nu)
                if tp_ == 0:
                    cev = K.op("dve", "tensor_copy", dst, srcv, deps=[("pe", ctr)])
                    ptr_ev2[0] = (cev, ptr_ev2[0][1])
                else:
                    cev = K.op("sc", "activation", dst, srcv, AF.Copy,
                               deps=[("pe", ctr)])
                    ptr_ev2[1] = (ptr_ev2[1][0], cev)

        prev = None
        for itl in range(4):
            K.op("dve", "reduce_max", s_cols.ap()[:, itl:itl + 1],
                 p2_mp.ap()[:, itl, 0:NSUP], axis=mybir.AxisListType.X,
                 deps=[("dve", mx_dep[itl]), ("sc", exp_cnt2[itl])])
            cv = K.op("dve", "tensor_scalar_mul", s_cols.ap()[:, 8 + itl:9 + itl],
                      s_cols.ap()[:, itl:itl + 1], -inv_sd)
            pb = itl % 2
            ce = K.op("sc", "activation", p2_P[pb].ap(), p2_S[itl].ap(), AF.Exp,
                      bias=s_cols.ap()[:, 8 + itl:9 + itl], scale=inv_sd,
                      accum_out=s_cols.ap()[:, 16 + itl:17 + itl],
                      deps=[("dve", cv), ("pe", p_free2[pb]),
                            ("sc", ev_dep[itl])])
            exp_cnt2[itl] = ce
            if prev is not None:
                p2_tail(prev)
            prev = (itl, pb, ce)
        p2_tail(prev)
        cv = K.op("dve", "reciprocal", s_cols.ap()[:, 24:28], s_cols.ap()[:, 16:20],
                  deps=[("sc", K.cnt["sc"])])
        ctr = K.trpack([(PS2[0:1, it * 128:(it + 1) * 128],
                         s_cols.ap()[:, 24 + it:25 + it]) for it in range(4)],
                       c_identf.ap(), deps=[("dve", cv), ("pe", pv_done)])
        cv = K.op("dve", "tensor_copy", s_rrow.ap(), PS2[0:1, :], deps=[("pe", ctr)])
        K.op("dve", "tensor_copy", r_hiB.ap(), s_rrow.ap())
        K.op("dve", "tensor_sub", r_t0.ap()[0:1, :], s_rrow.ap(), r_hiB.ap())
        cl = K.op("dve", "tensor_copy", r_loB.ap(), r_t0.ap()[0:1, :])
        cb = K.mmacc(PS3, [(c_onesbrow.ap(), r_hiB.ap()),
                           (c_onesbrow.ap(), r_loB.ap())], deps=[("dve", cl)])
        cbr = K.op("dve", "tensor_copy", s_Brinv.ap(), PS3, deps=[("pe", cb)])
        for ec in range(8):
            xb_ = ec % 2
            lx = K.dmaL(ec, p2_xs[xb_].ap(), xT[:, ec, s0:s0 + 512],
                        deps=[("dve", K.cnt["dve"])])
            ps = PS2 if ec % 2 == 0 else PS3
            cm = K.mmacc(ps, [(p1_V.ap()[:, jt, ec * 128:(ec + 1) * 128],
                               p2_PT.ap()[:, jt, :]) for jt in range(NT)],
                         deps=[("dve", K.cnt["dve"]), ("sc", K.cnt["sc"])])
            cv = K.op("dve", "tensor_mul", p2_tmp.ap(), ps, s_Brinv.ap(),
                      deps=[("pe", cm), ("dve", cbr)])
            cv = K.op("dve", "scalar_tensor_tensor", p2_x1[xb_].ap(), p2_tmp.ap(),
                      c_wvB.ap()[:, ec:ec + 1], p2_xs[xb_].ap(), ALU.add, ALU.add,
                      deps=[lx, ("qsS", K.cnt["qsS"])])
            K.dma("qsS", hX1[:, ec, s0:s0 + 512], p2_x1[xb_].ap(), deps=[("dve", cv)])
        pv_done = K.cnt["pe"]
    K.barrier()

    lb_ = K.dmaW(0, p1_xbf[0].ap(), hX1[:, :, 0:512],
                 deps=[("sc", K.cnt["sc"]), ("pe", K.cnt["pe"])])
    for c in range(NCH):
        c0 = c * 512
        cb = ln_stats(p1_xbf[c % 2].ap(), p1_sq.ap(), lb_)
        if c + 1 < NCH:
            lb_ = K.dmaW(c + 1, p1_xbf[(c + 1) % 2].ap(),
                         hX1[:, :, c0 + 512:c0 + 1024],
                         deps=[("sc", K.cnt["sc"]), ("pe", K.cnt["pe"])])
        for dt in range(8):
            lf = K.dmaL(dt, p1_xfd[dt % 2].ap(), hX1[:, dt, c0:c0 + 512],
                        deps=[("dve", K.cnt["dve"])])
            K.op("dve", "tensor_mul", p1_tmp.ap(), p1_xfd[dt % 2].ap(), s_B1.ap(),
                 deps=[lf, cb])
            cv = K.op("dve", "tensor_add", p3_y[0].ap(), p1_tmp.ap(), s_B2.ap(),
                      deps=[("qsS", K.cnt["qsS"])])
            K.dma("qsS", hY[:, dt, c0:c0 + 512], p3_y[0].ap(), deps=[("dve", cv)])
    K.barrier()

    for l in range(nlayers):
        K.dma("qsC", c_qsc.ap(), QSC[l], deps=[("dve", K.cnt["dve"])])
        K.dma("qsC", c_ksc.ap(), KSC[l], deps=[])
        K.dma("qsC", c_qscS.ap(), QSCS[l]); K.dma("qsC", c_kscS.ap(), KSCS[l])
        K.dma("qsC", p4_msk.ap()[:, 4:8, :], maskAdd[:, 4:8, :],
              deps=[("dve", K.cnt["dve"]),
                                                 ("sc", K.cnt["sc"]),
                                                 ("pe", K.cnt["pe"])])
        K.dma("qsC", p4_me.ap(), maskEdge)
        K.dma("qsC", c_lnG.ap(), LNG[l]); K.dma("qsC", c_lnB.ap(), LNB[l])
        K.dma("qsC", c_b1a.ap(), B1A[l]); K.dma("qsC", c_b1g.ap(), B1G[l])
        cpl = K.dma("qsC", c_b2.ap(), B2C[l])
        K.dma("qgS", p4_wqkv.ap(), WQKV[l], deps=[("pe", K.cnt["pe"])])
        K.dma("qgS", p4_wo.ap(), WO[l])
        cz1 = K.op("gp", "memset", p4_Kp.ap()[:, :, 0:WIN], 0.0,
                   deps=[("pe", K.cnt["pe"])])
        cz2 = K.op("gp", "memset", p4_Vp.ap()[:, 0:4, :], 0.0)

        qk_done = [(0, 0, 0)] * 2
        esel_pe = [K.cnt["pe"]]

        hload = {}

        def p4a_load(c):
            c0 = c * 512
            hb = c % 2
            hs = p4_hs[hb]
            dep_pe = ("pe", K.cnt["pe"])
            K.dmaW(c, hs.ap()[:, 0:4, :], hY[:, 0:4, c0:c0 + 512], deps=[dep_pe])
            if c0 == 0:
                K.op("gp", "memset", hs.ap()[:, 4:8, 0:1], 0.0, deps=[dep_pe])
                l2 = K.dmaW(c, hs.ap()[:, 4:8, 1:512], hY[:, 4:8, 0:511])
            else:
                l2 = K.dmaW(c, hs.ap()[:, 4:8, :], hY[:, 4:8, c0 - 1:c0 + 511])
            hload[c] = (l2, ("gp", K.cnt["gp"]))

        def p4a_s1(c):
            c0 = c * 512
            hb = c % 2
            hs = p4_hs[hb]
            dep_h = list(hload[c])
            scq, dvq, pesw = qk_done[hb]
            for qk, dst in ((0, p4_qc[hb]), (1, p4_kc2[hb])):
                for tile in range(4):
                    ecol = qk * 512 + tile * 128
                    ps = PS2 if tile % 2 == 0 else PS3
                    cm = K.mmacc(ps, [(p4_wqkv.ap()[:, dt, ecol:ecol + 128],
                                       hs.ap()[:, dt, :]) for dt in range(8)],
                                 deps=dep_h + [("qgS", K.cnt["qgS"]),
                                               ("sc", K.cnt["sc"]),
                                               ("dve", K.cnt["dve"])])
                    K.op("sc", "activation", dst.ap()[:, tile, :], ps, AF.Copy,
                         deps=[("pe", cm), ("sc", scq), ("dve", dvq),
                               ("pe", pesw)])
            for tt2 in range(4):
                ps = PS2 if tt2 % 2 == 0 else PS3
                cm = K.mmacc(ps, [(hs.ap()[:, dt, tt2 * 128:(tt2 + 1) * 128],
                                   p4_wqkv.ap()[:, dt, 1024:1536])
                                  for dt in range(8)],
                             deps=[("sc", K.cnt["sc"]), ("dve", K.cnt["dve"])])
                K.op("sc", "activation", p4_Vp.ap()[:, 4 + c * 4 + tt2, :], ps,
                     AF.Copy, deps=[("pe", cm)])

        def p4a_s2(c):
            c0 = c * 512
            hb = c % 2
            for src_, rowdst in ((p4_qc[hb], s_r8a), (p4_kc2[hb], s_r8b)):
                csq = K.op("sc", "activation", p4_sq.ap(), src_.ap(), AF.Square,
                           deps=[("pe", esel_pe[0])])
                cm = K.mmacc(PS2[0:8, :], [(c_Esel.ap()[:, tile, :],
                                            p4_sq.ap()[:, tile, :])
                                           for tile in range(4)],
                             deps=[("sc", csq), ("qsC", consts_done),
                                   ("dve", K.cnt["dve"])])
                esel_pe[0] = cm
                K.op("dve", "tensor_scalar_max", s_r8c.ap(), PS2[0:8, :], 1e-24,
                     deps=[("pe", cm)])
                cs = K.op("sc", "activation", rowdst.ap(), s_r8c.ap(), AF.Sqrt,
                          deps=[("dve", K.cnt["dve"])])
                K.op("dve", "reciprocal", rowdst.ap(), rowdst.ap(), deps=[("sc", cs)])
                K.op("dve", "tensor_mul", r_t0.ap()[0:8, :], rowdst.ap(), rowdst.ap())
                K.op("dve", "tensor_mul", r_t1.ap()[0:8, :], s_r8c.ap(),
                     r_t0.ap()[0:8, :])
                K.op("dve", "tensor_scalar", r_t1.ap()[0:8, :], r_t1.ap()[0:8, :],
                     -0.5, 1.5, ALU.mult, ALU.add)
                K.op("dve", "tensor_mul", rowdst.ap(), rowdst.ap(), r_t1.ap()[0:8, :])
            K.op("dve", "tensor_scalar_mul", s_r8b.ap(), s_r8b.ap(), 8.0)
            cv = K.op("dve", "tensor_copy", p4_rkb.ap(), s_r8b.ap())
            cvq = K.op("dve", "tensor_copy", p4_rqb.ap(), s_r8a.ap())
            for ti, (tab, scl) in enumerate(((XQ1, c_qsc), (XQ2, c_qscS),
                                             (XK1, c_ksc), (XK2, c_kscS))):
                lt = K.dmaL(ti, p4_tf[ti % 2].ap(), tab[:, c0:c0 + 512],
                            deps=[("dve", K.cnt["dve"])])
                K.op("dve", "tensor_scalar_mul", p4_tb.ap()[:, ti, :],
                     p4_tf[ti % 2].ap(), scl.ap(),
                     deps=[lt, ("qsC", K.cnt["qsC"])])
            last_sw = 0
            for tile in range(4):
                csw = K.mmacc(PS3, [(c_swap32.ap(), p4_qc[hb].ap()[:, tile, :])],
                              deps=[("dve", K.cnt["dve"]), ("sc", K.cnt["sc"]),
                                    ("qsC", K.cnt["qsC"])])
                K.op("dve", "tensor_mul", p4_m1.ap(), p4_qc[hb].ap()[:, tile, :],
                     p4_tb.ap()[:, 0, :])
                K.op("dve", "tensor_mul", p4_m2.ap(), PS3, p4_tb.ap()[:, 1, :],
                     deps=[("pe", csw)])
                cmq = K.mmacc(PS2, [(c_Bsel.ap()[:, tile * 128:(tile + 1) * 128],
                                     p4_rqb.ap())],
                              deps=[("dve", K.cnt["dve"])])
                cbq = K.op("dve", "tensor_copy", p4_brk.ap(), PS2, deps=[("pe", cmq)])
                K.op("dve", "tensor_add", p4_m1.ap(), p4_m1.ap(), p4_m2.ap(),
                     deps=[("qsS", K.cnt["qsS"])])
                cv = K.op("dve", "tensor_mul", p4_qh.ap(), p4_m1.ap(), p4_brk.ap())
                K.dma("qsS", hQi[:, tile, c0:c0 + 512], p4_qh.ap(), deps=[("dve", cv)])
                cm = K.mmacc(PS2, [(c_Bsel.ap()[:, tile * 128:(tile + 1) * 128],
                                    p4_rkb.ap())],
                             deps=[("dve", K.cnt["dve"])])
                cbk = K.op("dve", "tensor_copy", p4_brk.ap(), PS2, deps=[("pe", cm)])
                csw = K.mmacc(PS3, [(c_swap32.ap(), p4_kc2[hb].ap()[:, tile, :])],
                              deps=[("dve", K.cnt["dve"])])
                last_sw = csw
                K.op("dve", "tensor_mul", p4_m1.ap(), p4_kc2[hb].ap()[:, tile, :],
                     p4_tb.ap()[:, 2, :])
                K.op("dve", "tensor_mul", p4_m2.ap(), PS3, p4_tb.ap()[:, 3, :],
                     deps=[("pe", csw)])
                K.op("dve", "tensor_add", p4_m1.ap(), p4_m1.ap(), p4_m2.ap())
                cvl = K.op("dve", "tensor_mul",
                           p4_Kp.ap()[:, tile, WIN + c0:WIN + c0 + 512],
                           p4_m1.ap(), p4_brk.ap(), deps=[("gp", cz1)])
            qk_done[hb] = (K.cnt["sc"], K.cnt["dve"], last_sw)

        p4a_load(0)
        if NCH > 1:
            p4a_load(1)
        p4a_s1(0)
        for c in range(NCH):
            if c + 1 < NCH:
                p4a_s1(c + 1)
            if c + 2 < NCH:
                p4a_load(c + 2)
            p4a_s2(c)
        K.barrier()

        def p4_wo_part(sprev, dts):
            sp0 = sprev * 512
            ob = sprev % 2
            for dt in dts:
                yb_ = dt % 2
                ps = PS2 if dt % 2 == 0 else PS3
                cm = K.mmacc(ps, [(p4_wo.ap()[:, k, dt * 128:(dt + 1) * 128],
                                   p4_Og[ob].ap()[:, k, :]) for k in range(4)],
                             deps=[("qgS", K.cnt["qgS"]),
                                   ("dve", K.cnt["dve"]), ("sc", K.cnt["sc"])])
                cv = K.op("dve", "tensor_add", p4_y1[yb_].ap(), ps,
                          p4_ysf[ob].ap()[:, dt, :],
                          deps=[("pe", cm), ("qsL", K.cnt["qsL"]),
                                ("qsL2", K.cnt["qsL2"]),
                                ("qsS", K.cnt["qsS"])])
                K.dma("qsS", hY[:, dt, sp0 + 0:sp0 + 512] if False else
                      hY[:, dt, sp0:sp0 + 512], p4_y1[yb_].ap(),
                      deps=[("dve", cv)])

        for s in range(NSUP):
            s0 = s * 512
            qb = s % 2
            lq = K.dmaL(s, p4_qi[qb].ap(), hQi[:, :, s0:s0 + 512],
                        deps=[("pe", K.cnt["pe"])])
            K.dmaL(s, p4_ysf[s % 2].ap(), hY[:, :, s0:s0 + 512],
                   deps=[("dve", K.cnt["dve"])])
            psh_sc = [0, 0]
            psh_dve = [0, 0]
            p_free = [0, 0, 0, 0]
            ptr_ev = [(0, 0), (0, 0)]
            for hp in range(4):
                pt_dep = K.cnt["pe"]
                prev = None

                def emit_pair(it2):
                    bl = 0 if s == 0 else it2 * 128
                    b0 = s0 + bl
                    n2 = 512 if s == 0 else 128
                    K.wait("pe", [lq, ("sc", psh_sc[0]), ("dve", psh_dve[0]),
                                  ("sc", psh_sc[1]), ("dve", psh_dve[1]),
                                  ("pe", K.cnt["pe"])])
                    for c0_, n_ in ((0, 512), (512, n2)):
                        for hh in range(2):
                            PSH = PS0 if hh == 0 else PS1
                            ins = kb.e["pe"].matmul(
                                PSH[:, c0_:c0_ + n_],
                                p4_qi[qb].ap()[64 * hh:64 * hh + 64, hp,
                                               it2 * 128:(it2 + 1) * 128],
                                p4_Kp.ap()[64 * hh:64 * hh + 64, hp,
                                           b0 + c0_:b0 + c0_ + n_],
                                start=True, stop=True,
                                tile_position=(64 * hh, 0))
                            kb.cnt["pe"] += 1
                            ins.then_inc(kb.sem["pe"], 1)
                    return kb.cnt["pe"]

                def emit_scores(it2, hh, cm2):
                    h = 2 * hp + hh
                    PSH = PS0 if hh == 0 else PS1
                    hcol = hh * 4 + it2
                    bl = 0 if s == 0 else it2 * 128
                    bw = 1024 if s == 0 else 640
                    if s == 0:
                        K.op("dve", "tensor_add", PSH, PSH,
                             p4_msk.ap()[:, 4 + it2, :], deps=[("pe", cm2)])
                    else:
                        K.op("dve", "tensor_tensor",
                             PSH.rearrange("p (u n) -> p u n", u=8)[:, 0:8:4, :],
                             PSH.rearrange("p (u n) -> p u n", u=8)[:, 0:8:4, :],
                             p4_me.ap(), ALU.add, deps=[("pe", cm2)])
                    cnb = K.op("dve", "reduce_max",
                               p4_nb.ap()[:, hcol:hcol + 1], PSH[:, 0:bw],
                               axis=mybir.AxisListType.X, negate=True)
                    psh_dve[hh] = cnb
                    pb = (it2 * 2 + hh) % 4
                    ce = K.op("sc", "activation",
                              p4_P2[pb].ap()[:, bl:bl + bw], PSH[:, 0:bw],
                              AF.Exp, bias=p4_nb.ap()[:, hcol:hcol + 1],
                              accum_out=p4_rsb.ap()[:, hcol:hcol + 1],
                              deps=[("dve", cnb), ("pe", p_free[pb])])
                    psh_sc[hh] = ce
                    return (it2, hh, pb, ce)

                def emit_tail(info):
                    it2, hh, pb, ce = info
                    tp_ = (it2 * 2 + hh) % 2
                    ptr = PTR0 if tp_ == 0 else PTR1
                    jt0 = 0 if s == 0 else it2
                    njt = 8 if s == 0 else 5
                    ctr = K.trpack(
                        [(ptr[:, u * 128:(u + 1) * 128],
                          p4_P2[pb].ap()[:, (jt0 + u) * 128:(jt0 + u + 1) * 128])
                         for u in range(njt)],
                        c_identbf.ap(),
                        deps=[("sc", ce), ("dve", ptr_ev[tp_][0]),
                              ("sc", ptr_ev[tp_][1]), ("pe", pt_dep)])
                    p_free[pb] = ctr
                    dst = p4_PT[hh].ap()[:, jt0:jt0 + njt,
                                         it2 * 128:(it2 + 1) * 128]
                    srcv = ptr[:, 0:njt * 128].rearrange("p (u n) -> p u n", u=njt)
                    if tp_ == 0:
                        cev = K.op("dve", "tensor_copy", dst, srcv,
                                   deps=[("pe", ctr)])
                        ptr_ev[0] = (cev, ptr_ev[0][1])
                    else:
                        cev = K.op("sc", "activation", dst, srcv, AF.Copy,
                                   deps=[("pe", ctr)])
                        ptr_ev[1] = (ptr_ev[1][0], cev)
                    if s != 0:
                        if it2 > 0:
                            K.op("gp", "memset",
                                 p4_PT[hh].ap()[:, 0:it2,
                                                it2 * 128:(it2 + 1) * 128],
                                 0.0, deps=[("pe", pt_dep)])
                        if it2 + 5 < 8:
                            K.op("gp", "memset",
                                 p4_PT[hh].ap()[:, it2 + 5:8,
                                                it2 * 128:(it2 + 1) * 128],
                                 0.0, deps=[("pe", pt_dep)])

                pend2 = []
                for it2 in range(4):
                    cm2 = emit_pair(it2)
                    pend2.append(emit_scores(it2, 0, cm2))
                    pend2.append(emit_scores(it2, 1, cm2))
                    if it2 > 0:
                        emit_tail(pend2.pop(0))
                        emit_tail(pend2.pop(0))
                for info in pend2:
                    emit_tail(info)
                cv = K.op("dve", "reciprocal", p4_rsi.ap(), p4_rsb.ap(),
                          deps=[("sc", K.cnt["sc"])])
                ctr = K.trpack([(PS2[0:1, it * 128:(it + 1) * 128],
                                 p4_rsi.ap()[:, it:it + 1]) for it in range(4)],
                               c_identf.ap(), deps=[("dve", cv)])
                cv = K.op("dve", "tensor_copy", s_rrow.ap(), PS2[0:1, :],
                          deps=[("pe", ctr)])
                ctr = K.trpack([(PS2[0:1, it * 128:(it + 1) * 128],
                                 p4_rsi.ap()[:, 4 + it:5 + it]) for it in range(4)],
                               c_identf.ap(), deps=[("dve", cv)])
                cv2 = K.op("dve", "tensor_copy", s_rrow2.ap(), PS2[0:1, :],
                           deps=[("pe", ctr)])
                cb = K.mmacc(PS3, [(c_e0.ap(), s_rrow.ap()),
                                   (c_e1.ap(), s_rrow2.ap())],
                             deps=[("dve", cv2)])
                cbr = K.op("dve", "tensor_copy", s_Brinv.ap(), PS3, deps=[("pe", cb)])
                K.wait("pe", [("dve", K.cnt["dve"]), ("sc", K.cnt["sc"]),
                              ("gp", K.cnt["gp"]), ("pe", K.cnt["pe"])])
                ins = None
                for jt in range(8):
                    for hh in range(2):
                        PSV = PS2 if hh == 0 else PS3
                        ins = kb.e["pe"].matmul(
                            PSV[64 * hh:64 * hh + 64, :],
                            p4_Vp.ap()[:, s * 4 + jt,
                                       (2 * hp + hh) * 64:(2 * hp + hh) * 64 + 64],
                            p4_PT[hh].ap()[:, jt, :],
                            start=(jt == 0), stop=(jt == 7),
                            tile_position=(0, 64 * hh))
                kb.cnt["pe"] += 1
                ins.then_inc(kb.sem["pe"], 1)
                cpv = kb.cnt["pe"]
                K.op("dve", "tensor_mul", p4_Og[s % 2].ap()[0:64, hp, :],
                     PS2[0:64, :], s_Brinv.ap()[0:64, :],
                     deps=[("pe", cpv), ("dve", cbr)])
                K.op("dve", "tensor_mul", p4_Og[s % 2].ap()[64:128, hp, :],
                     PS3[64:128, :], s_Brinv.ap()[64:128, :],
                     deps=[("pe", cpv)])
                if s > 0:
                    p4_wo_part(s - 1, (2 * hp, 2 * hp + 1))
        p4_wo_part(NSUP - 1, range(8))
        K.barrier()

        w2done = K.cnt["pe"]
        w2ev = [(K.cnt["sc"], K.cnt["dve"])]
        w2ld_hold = [None]
        if NC6 > 1:
            for cb_ in range(1, NC6):
                K.dma("qsC", p6_bnd.ap()[:, :, cb_ - 1:cb_],
                      hY[:, 4:8, cb_ * CH6 - 1:cb_ * CH6],
                      deps=[("dve", K.cnt["dve"]), ("pe", K.cnt["pe"]),
                            ("sc", K.cnt["sc"])],
                      allow_slow_non_contiguous=True)
        stats_pe6 = [K.cnt["pe"]]
        ptrB_free = [0]

        def p6_head(c):
            c0 = c * CH6
            dep_pe = ("pe", stats_pe6[0])
            l1 = K.dmaW(c, p6_hb.ap()[:, 0:4, :], hY[:, 0:4, c0:c0 + CH6],
                        deps=[dep_pe, ("sc", K.cnt["sc"])])
            if c0 == 0:
                K.op("gp", "memset", p6_hb.ap()[:, 4:8, 0:1], 0.0, deps=[dep_pe])
                l2 = K.dmaW(c, p6_hb.ap()[:, 4:8, 1:CH6], hY[:, 4:8, 0:CH6 - 1])
                return [l2, l1, ("gp", K.cnt["gp"])]
            l2 = K.dmaW(c, p6_hb.ap()[:, 4:8, 1:CH6],
                        hY[:, 4:8, c0:c0 + CH6 - 1])
            cbd = K.op("dve", "tensor_copy", p6_hb.ap()[:, 4:8, 0:1],
                       p6_bnd.ap()[:, :, c - 1:c],
                       deps=[("qsC", K.cnt["qsC"]), ("sc", K.cnt["sc"]),
                             dep_pe])
            return [l2, l1, ("gp", K.cnt["gp"]), ("dve", cbd)]

        def p6_lnphase(c, dep_h):
            c0 = c * CH6
            lnb = p6_lnd[c % 2]
            sq_pe = stats_pe6[0]
            NS6 = CH6 // 512
            binfo = []
            for sub in range(NS6):
                u0 = sub * 512
                hbs = p6_hb.ap()[:, :, u0:u0 + 512]
                csq = K.op("sc", "activation", p6_sq.ap(), hbs, AF.Square,
                           deps=list(dep_h) + [("pe", sq_pe)])
                if sub == 0:
                    cb = ln_stats(hbs, p6_sq.ap(), dep_h[0], csq=csq)
                    binfo.append((cb, s_B1.ap(), s_B2.ap()))
                else:
                    cb = ln_stats(hbs, p6_sq.ap(), dep_h[0], csq=csq,
                                  bdst=(PTR0f, PTR1f),
                                  bextra=[("dve", ptrB_free[0])])
                    binfo.append((cb, PTR0f, PTR1f))
                sq_pe = K.cnt["pe"]
            for sub in range(NS6):
                u0 = sub * 512
                cb, B1a, B2a = binfo[sub]
                for dt in range(8):
                    fb = dt % 2
                    if u0 == 0 and dt >= 4:
                        if c0 == 0:
                            K.op("gp", "memset", p6_hf[fb].ap()[:, 0:1], 0.0,
                                 deps=[("dve", K.cnt["dve"])])
                        else:
                            K.op("dve", "tensor_copy", p6_hf[fb].ap()[:, 0:1],
                                 p6_bnd.ap()[:, dt - 4, c - 1:c],
                                 deps=[("qsC", K.cnt["qsC"]),
                                       ("qsL" if dt % 2 == 0 else "qsL2",
                                        K.cnt["qsL" if dt % 2 == 0 else "qsL2"])])
                        lf = K.dmaL(dt, p6_hf[fb].ap()[:, 1:512],
                                    hY[:, dt, c0:c0 + 511],
                                    deps=[("dve", K.cnt["dve"])])
                    else:
                        off = c0 + u0 - (1 if dt >= 4 else 0)
                        lf = K.dmaL(dt, p6_hf[fb].ap(), hY[:, dt, off:off + 512],
                                    deps=[("dve", K.cnt["dve"])])
                    K.op("dve", "tensor_mul", p6_t1.ap(), p6_hf[fb].ap(), B1a,
                         deps=[lf, cb, ("gp", K.cnt["gp"])])
                    K.op("dve", "tensor_add", p6_t1.ap(), p6_t1.ap(), B2a)
                    K.op("dve", "tensor_scalar", lnb.ap()[:, dt, u0:u0 + 512],
                         p6_t1.ap(), c_lnG.ap()[:, dt:dt + 1],
                         c_lnB.ap()[:, dt:dt + 1], ALU.mult, ALU.add,
                         deps=[("qsC", K.cnt["qsC"])])
                if sub == NS6 - 1 and NS6 > 1:
                    ptrB_free[0] = K.cnt["dve"]
            stats_pe6[0] = K.cnt["pe"]
            return ("dve", K.cnt["dve"])

        def p6_mms(c, dep_ln):
            nonlocal w2done
            c0 = c * CH6
            lnb = p6_lnd[c % 2]
            if c == 0:
                for k in range(KT):
                    K.dmaW(k, p6_W2.ap()[:, k, :], W2P[l][k],
                           deps=[("pe", w2done)])
                w2ld_hold[0] = [("qgW", K.cnt["qgW"]),
                                ("qgW2", K.cnt["qgW2"])]
            w2_ld = w2ld_hold[0]
            NSUB = CH6 // 512
            aev = [0, 0]
            gev = [0, 0]
            h4d = [0, 0]
            h4k = [[0] * KT, [0] * KT]
            lw1 = {}
            for k in range(min(2, KT)):
                K.dmaW(k, p6_w1a[k % 2].ap(), W1A[l][k], deps=[("pe", w2done)])
                lw1[k] = K.dmaW(k, p6_w1g[k % 2].ap(), W1G[l][k])
            for k in range(KT):
                wb = k % 2
                cmg = 0
                for sub in range(NSUB):
                    u0 = sub * 512
                    psa = PS0a if sub == 0 else PS0b
                    psg = PS1a if sub == 0 else PS1b
                    extra = ([("sc", w2ev[0][0]), ("dve", w2ev[0][1])]
                             if k == 0 else [])
                    cma = K.mmacc(psa, [(p6_w1a[wb].ap()[:, dt, :],
                                         lnb.ap()[:, dt, u0:u0 + 512])
                                        for dt in range(8)],
                                  deps=[dep_ln, lw1[k], ("dve", aev[sub])] + extra)
                    cmg = K.mmacc(psg, [(p6_w1g[wb].ap()[:, dt, :],
                                         lnb.ap()[:, dt, u0:u0 + 512])
                                        for dt in range(8)],
                                  deps=[("sc", gev[sub])] + extra)
                    aev[sub] = K.op("dve", "tensor_scalar", p6_At[sub].ap(), psa,
                                    c_b1a.ap()[:, k:k + 1], None, ALU.add,
                                    deps=[("pe", cma), ("dve", h4d[sub])])
                    gev[sub] = K.op("sc", "activation",
                                    p6_G.ap()[:, k, u0:u0 + 512], psg, AF.Gelu,
                                    bias=c_b1g.ap()[:, k:k + 1],
                                    deps=[("pe", cmg)])
                    h4d[sub] = K.op("dve", "tensor_mul",
                                    p6_G.ap()[:, k, u0:u0 + 512],
                                    p6_At[sub].ap(),
                                    p6_G.ap()[:, k, u0:u0 + 512],
                                    deps=[("sc", gev[sub])])
                    h4k[sub][k] = h4d[sub]
                if k + 2 < KT:
                    K.dmaW(k, p6_w1a[wb].ap(), W1A[l][k + 2],
                           deps=[("pe", cmg)])
                    lw1[k + 2] = K.dmaW(k, p6_w1g[wb].ap(), W1G[l][k + 2])
            bank_sc = [0, 0, gev[0], gev[1]]
            bank_dve = [aev[0], aev[1], 0, 0]
            for sub in range(NSUB):
                u0 = sub * 512
                for dg in range(2):
                    for k in range(KT):
                        for d2 in range(4):
                            ps = (PS0a, PS0b, PS1a, PS1b)[d2]
                            K.wait("pe", w2_ld +
                                   [("pe", K.cnt["pe"]),
                                    ("sc", bank_sc[d2]),
                                    ("dve", max(bank_dve[d2],
                                                h4k[sub][k]))])
                            ins = kb.e["pe"].matmul(
                                ps, p6_W2.ap()[:, k, dg * 512 + d2 * 128:
                                               dg * 512 + (d2 + 1) * 128],
                                p6_G.ap()[:, k, u0:u0 + 512],
                                start=(k == 0), stop=(k == KT - 1))
                            if d2 == 3:
                                kb.cnt["pe"] += 1
                                ins.then_inc(kb.sem["pe"], 1)
                    cmw = K.cnt["pe"]
                    w2done = cmw
                    for d2 in range(4):
                        d = dg * 4 + d2
                        ps = (PS0a, PS0b, PS1a, PS1b)[d2]
                        yb_ = d2 % 2
                        ly = K.dmaL(d2, p6_ys[yb_].ap(),
                                    hY[:, d, c0 + u0:c0 + u0 + 512],
                                    deps=[("dve", K.cnt["dve"])])
                        cse = K.op("sc", "activation", p6_t1.ap(), ps, AF.Identity,
                                   bias=c_b2.ap()[:, d:d + 1],
                                   deps=[("pe", cmw), ("dve", K.cnt["dve"]),
                                         ("gp", K.cnt["gp"])])
                        bank_sc[d2] = cse
                        if l == 0:
                            cv = K.op("gp", "tensor_tensor", p6_ys[yb_].ap(),
                                      p6_t1.ap(), p6_ys[yb_].ap(), ALU.add,
                                      deps=[("sc", cse), ly,
                                            ("qsS", K.cnt["qsS"])])
                            K.dma("qsS", hY[:, d, c0 + u0:c0 + u0 + 512],
                                  p6_ys[yb_].ap(), deps=[("gp", cv)])
                        else:
                            lx = K.dmaL(d2, p6_xs[yb_].ap(),
                                        hX1[:, d, c0 + u0:c0 + u0 + 512],
                                        deps=[("gp", K.cnt["gp"])])
                            cv = K.op("gp", "tensor_tensor", p6_ys[yb_].ap(),
                                      p6_t1.ap(), p6_ys[yb_].ap(), ALU.add,
                                      deps=[("sc", cse), ly,
                                            ("qsS", K.cnt["qsS"])])
                            cv = K.op("gp", "tensor_tensor", p6_ys[yb_].ap(),
                                      p6_ys[yb_].ap(), p6_xs[yb_].ap(), ALU.add,
                                      deps=[lx])
                            K.dma("qsS", OUT[:, d, c0 + u0:c0 + u0 + 512],
                                  p6_ys[yb_].ap(), deps=[("gp", cv)])
            w2ev[0] = (K.cnt["sc"], K.cnt["dve"])

        dh = p6_head(0)
        cur_ln = p6_lnphase(0, dh)
        if NC6 > 1:
            dh1 = p6_head(1)
        for c in range(NC6):
            nxt_ln = None
            if c + 1 < NC6:
                nxt_ln = p6_lnphase(c + 1, dh1)
            p6_mms(c, cur_ln)
            if c + 2 < NC6:
                dh1 = p6_head(c + 2)
            cur_ln = nxt_ln
        K.barrier()

    K.barrier()
    return kb



def kernel(**inputs):
    x = np.asarray(inputs["L0_attn_x"])
    B, T, _ = x.shape
    shared = host_prep(inputs, T)
    nc = build_nc(T).nc
    in_maps = []
    for b in range(B):
        m = dict(shared)
        m["xT"] = prep_x(x[b])
        in_maps.append(m)
    res = run_bass_kernel_spmd(nc, in_maps, core_ids=list(range(B)))
    out = np.stack([unprep_out(r["OUT"]) for r in res.results], axis=0)
    return out.astype(np.float32)


# version 31
# speedup vs baseline: 1.0094x; 1.0094x over previous
import numpy as np
import ml_dtypes
import concourse.bass as bass
import concourse.mybir as mybir
from concourse.bass_utils import run_bass_kernel_spmd

F32 = mybir.dt.float32
BF16 = mybir.dt.bfloat16
AF = mybir.ActivationFunctionType
ALU = mybir.AluOpType
BF = ml_dtypes.bfloat16
NEG = -3.0e38

D = 1024
HEADS = 8
DH = 64
WIN = 512
HID = 2730
HIDP = 2816
KT = HIDP // 128


def _tile_rows(w):
    K = w.shape[0] // 128
    return np.ascontiguousarray(w.reshape(K, 128, -1).transpose(1, 0, 2))

def host_tables(T):
    t = {}
    inv = 1.0 / (10000.0 ** (np.arange(0, D, 2, dtype=np.float64) / D))
    ang = np.arange(T, dtype=np.float64)[None, :] * inv[:, None]
    t["ropeC"] = _tile_rows(np.cos(ang).astype(np.float32))
    t["ropeS"] = _tile_rows(np.sin(ang).astype(np.float32))
    half = np.arange(0, DH, 2, dtype=np.float64)
    invf = 1.0 / 10000.0 ** (half / DH)
    tt = np.arange(T, dtype=np.float64)
    fr = tt[:, None] * invf[None, :]
    freqs = np.concatenate([fr, fr], axis=1)
    base = (half + 0.4 * DH) / (1.4 * DH)
    power = (tt - T // 2) / float(WIN // 2)
    sc = base[None, :] ** power[:, None]
    scale = np.concatenate([sc, sc], axis=1)
    sign = np.where(np.arange(DH) < 32, -1.0, 1.0)
    cos, sin = np.cos(freqs), np.sin(freqs)
    rep = lambda a: np.ascontiguousarray(np.tile(a.T, (2, 1)).astype(np.float32))
    t["xq1"] = rep(cos * scale)
    t["xq2"] = rep(sign[None, :] * sin * scale)
    t["xk1"] = rep(cos / scale)
    t["xk2"] = rep(sign[None, :] * sin / scale)
    t["identbf"] = np.eye(128, dtype=np.float32).astype(BF)
    sw = np.zeros((128, 128), np.float32)
    for p in range(128):
        sw[p, p ^ 32] = 1.0
    t["swap32"] = sw.astype(BF)
    t["identf"] = np.eye(128, dtype=np.float32)
    t["onesbf"] = np.ones((128, 1), dtype=np.float32).astype(BF)
    t["onesfrow"] = np.ones((1, 128), dtype=np.float32)
    t["onesbrow"] = np.ones((1, 128), dtype=ml_dtypes.bfloat16)
    e0 = np.zeros((1, 128), np.float32); e0[0, :64] = 1.0
    e1 = np.zeros((1, 128), np.float32); e1[0, 64:] = 1.0
    t["e0row"], t["e1row"] = e0, e1
    es = np.zeros((128, 4, 8), np.float32)
    for tile in range(4):
        for p in range(128):
            es[p, tile, 2 * tile + p // 64] = 1.0
    t["Esel"] = es.astype(BF)
    bs = np.zeros((8, 4, 128), np.float32)
    for tile in range(4):
        for p in range(128):
            bs[2 * tile + p // 64, tile, p] = 1.0
    t["Bsel"] = np.ascontiguousarray(bs.reshape(8, 512)).astype(BF)
    r = np.arange(128)[:, None]
    cc = np.arange(1024)[None, :]
    madd = np.zeros((128, 8, 1024), np.float32)
    for itp in range(4):
        st = r + 128 * itp
        keep = (cc >= st) & (cc < st + 513)
        madd[:, itp, :] = np.where(keep, 0.0, -1e30)
        keep0 = keep & (cc >= 512)
        madd[:, 4 + itp, :] = np.where(keep0, 0.0, -1e30)
    t["maskAdd"] = madd.astype(BF)
    r1 = np.arange(128)[:, None]
    c1 = np.arange(128)[None, :]
    me = np.zeros((128, 2, 128), np.float32)
    me[:, 0, :] = np.where(c1 >= r1, 0.0, -1e30)
    me[:, 1, :] = np.where(c1 <= r1, 0.0, -1e30)
    t["maskEdge"] = me.astype(BF)
    return t


def host_prep(inputs, T):
    g = lambda k: np.asarray(inputs[k])
    m = dict(host_tables(T))
    perm = np.concatenate([np.arange(0, D, 2), np.arange(1, D, 2)])
    f32 = np.float32
    m["wqT"] = _tile_rows(np.ascontiguousarray(g("L0_attn_wq_w").T[:, perm])).astype(f32)
    m["wkT"] = _tile_rows(np.ascontiguousarray(g("L0_attn_wk_w").T[:, perm])).astype(f32)
    m["wvT"] = _tile_rows(np.ascontiguousarray(g("L0_attn_wv_w").T)).astype(f32)
    m["wqB"] = np.ascontiguousarray(g("L0_attn_wq_b")[perm].reshape(8, 128).T).astype(f32)
    m["wkB"] = np.ascontiguousarray(g("L0_attn_wk_b")[perm].reshape(8, 128).T).astype(f32)
    m["wvB"] = np.ascontiguousarray(g("L0_attn_wv_b").reshape(8, 128).T).astype(f32)
    wqkv = g("L0_ffn_wqkv"); wo = g("L0_ffn_wo")
    qsc = g("L0_ffn_q_scale"); ksc = g("L0_ffn_k_scale")
    lg = g("L0_ffn_ln_g"); lb = g("L0_ffn_ln_b")
    w1 = g("L0_ffn_w1"); b1 = g("L0_ffn_b1")
    w2 = g("L0_ffn_w2"); b2 = g("L0_ffn_b2")
    for l in range(2):
        m[f"wqkv{l}"] = _tile_rows(wqkv[l]).astype(f32)
        m[f"wo{l}"] = _tile_rows(wo[l]).astype(f32)
        m[f"qsc{l}"] = np.tile(qsc[l], 2).reshape(128, 1).astype(f32)
        m[f"ksc{l}"] = np.tile(ksc[l], 2).reshape(128, 1).astype(f32)
        swp = np.arange(64) ^ 32
        m[f"qscS{l}"] = np.tile(qsc[l][swp], 2).reshape(128, 1).astype(f32)
        m[f"kscS{l}"] = np.tile(ksc[l][swp], 2).reshape(128, 1).astype(f32)
        m[f"lnG{l}"] = np.ascontiguousarray(lg[l].reshape(8, 128).T).astype(f32)
        m[f"lnB{l}"] = np.ascontiguousarray(lb[l].reshape(8, 128).T).astype(f32)
        w1a = np.zeros((D, HIDP), f32); w1a[:, :HID] = w1[l][:, :HID]
        w1g = np.zeros((D, HIDP), f32); w1g[:, :HID] = w1[l][:, HID:]
        m[f"w1a{l}"] = np.ascontiguousarray(
            _tile_rows(w1a).reshape(128, 8, KT, 128).transpose(2, 0, 1, 3))
        m[f"w1g{l}"] = np.ascontiguousarray(
            _tile_rows(w1g).reshape(128, 8, KT, 128).transpose(2, 0, 1, 3))
        b1a = np.zeros(HIDP, f32); b1a[:HID] = b1[l][:HID]
        b1g = np.zeros(HIDP, f32); b1g[:HID] = b1[l][HID:]
        m[f"b1a{l}"] = np.ascontiguousarray(b1a.reshape(KT, 128).T)
        m[f"b1g{l}"] = np.ascontiguousarray(b1g.reshape(KT, 128).T)
        w2p = np.zeros((HIDP, D), f32); w2p[:HID] = w2[l]
        m[f"w2p{l}"] = np.ascontiguousarray(_tile_rows(w2p).transpose(1, 0, 2))
        m[f"b2c{l}"] = np.ascontiguousarray(b2[l].reshape(8, 128).T).astype(f32)
    return m


def prep_x(xb):
    return np.ascontiguousarray(xb.T.reshape(8, 128, -1).transpose(1, 0, 2))

def unprep_out(o):
    return np.ascontiguousarray(o.transpose(1, 0, 2).reshape(1024, -1).T)



STREAM_OF = {"pe": "P", "dve": "V", "sc": "S", "gp": "G",
             "qsC": "Q", "qsL": "Q", "qsL2": "Q", "qsS": "Q", "qsS2": "Q",
             "qgW": "G", "qgW2": "G", "qgS": "G"}
SELF_WAIT = ("pe", "dve", "sc", "gp")

class KB:
    def __init__(self, T):
        self.T = T
        self.nc = bass.Bass("TRN2", target_bir_lowering=False, debug=False)
        n = self.nc
        self.e = {"pe": n.tensor, "dve": n.vector, "sc": n.scalar,
                  "gp": n.gpsimd, "qsC": n.sync, "qsL": n.sync,
                  "qsL2": n.sync, "qsS": n.sync, "qsS2": n.sync,
                  "qgW": n.gpsimd,
                  "qgW2": n.gpsimd, "qgS": n.gpsimd}
        self.sem = {k: n.alloc_semaphore("sem_" + k) for k in self.e}
        self.cnt = {k: 0 for k in self.e}
        self.step = {"pe": 1, "dve": 1, "sc": 1, "gp": 1,
                     "qsC": 16, "qsL": 16, "qsL2": 16, "qsS": 16,
                     "qsS2": 16, "qgW": 16, "qgW2": 16, "qgS": 16}
        self.lastw = {s: {k: 0 for k in self.e} for s in "PVSGQ"}

    def wait(self, k, deps):
        st = STREAM_OF[k]
        for (j, v) in deps:
            if v is None or v <= 0:
                continue
            if self.lastw[st][j] >= v:
                continue
            self.e[k].wait_ge(self.sem[j], v)
            self.lastw[st][j] = v

    def op(self, k, name, *a, deps=(), inc=True, **kw):
        deps = list(deps)
        if k in SELF_WAIT:
            deps.append((k, self.cnt[k]))
        self.wait(k, deps)
        ins = getattr(self.e[k], name)(*a, **kw)
        if inc:
            self.cnt[k] += self.step[k]
            ins.then_inc(self.sem[k], self.step[k])
            return self.cnt[k]
        return None

    def dma(self, k, out, in_, deps=(), **kw):
        c = self.op(k, "dma_start", out, in_, deps=deps, **kw)
        return (k, c)

    def dmaL(self, i, out, in_, deps=()):
        k = "qsL" if i % 2 == 0 else "qsL2"
        return self.dma(k, out, in_, deps=list(deps) + [(k, self.cnt[k])])

    def dmaW(self, i, out, in_, deps=()):
        k = "qgW" if i % 2 == 0 else "qgW2"
        return self.dma(k, out, in_, deps=list(deps) + [(k, self.cnt[k])])

    def mmacc(self, out, pairs, deps=(), tile_position=None, transpose=False):
        deps = list(deps) + [("pe", self.cnt["pe"])]
        self.wait("pe", deps)
        n = len(pairs)
        ins = None
        for i, (l, r) in enumerate(pairs):
            kw = dict(start=(i == 0), stop=(i == n - 1))
            if tile_position is not None:
                kw["tile_position"] = tile_position
            if transpose:
                kw["is_transpose"] = True
            ins = self.e["pe"].matmul(out, l, r, **kw)
        self.cnt["pe"] += 1
        ins.then_inc(self.sem["pe"], 1)
        return self.cnt["pe"]

    def trpack(self, outs_ins, ident, deps=()):
        deps = list(deps) + [("pe", self.cnt["pe"])]
        self.wait("pe", deps)
        n = len(outs_ins)
        ins = None
        for i, (o, s) in enumerate(outs_ins):
            ins = self.e["pe"].matmul(o, s, ident, is_transpose=True,
                                      start=(i == 0), stop=(i == n - 1))
        self.cnt["pe"] += 1
        ins.then_inc(self.sem["pe"], 1)
        return self.cnt["pe"]

    def qdep(self, ch):
        return (ch, self.cnt[ch])

    def barrier(self):
        for q in ("qsC", "qsL", "qsL2", "qsS", "qsS2", "qgW", "qgW2",
                  "qgS"):
            self.wait(q, [(q, self.cnt[q])])
        self.nc.all_engine_barrier()
        for s in "PVSGQ":
            for k in self.e:
                self.lastw[s][k] = max(self.lastw[s][k], self.cnt[k])

    def allc(self, k):
        return (k, self.cnt[k])



def build_nc(T=4096, dbg=False, nlayers=2):
    kb = KB(T)
    K = kb
    nc = kb.nc
    NCH = T // 512
    NSUP = T // 512
    NT = T // 128
    NC6 = max(1, T // 1024)
    CH6 = T // NC6

    di = lambda nm, sh, dt=F32: nc.dram_tensor(nm, sh, dt, kind="ExternalInput").ap()
    dkind = "ExternalOutput" if dbg else "Internal"
    xT = di("xT", [128, 8, T])
    OUT = nc.dram_tensor("OUT", [128, 8, T], F32, kind="ExternalOutput").ap()
    wqT = di("wqT", [128, 8, 1024]); wkT = di("wkT", [128, 8, 1024])
    wvT = di("wvT", [128, 8, 1024])
    wqB = di("wqB", [128, 8]); wkB = di("wkB", [128, 8]); wvB = di("wvB", [128, 8])
    ropeC = di("ropeC", [128, 4, T]); ropeS = di("ropeS", [128, 4, T])
    XQ1 = di("xq1", [128, T]); XQ2 = di("xq2", [128, T])
    XK1 = di("xk1", [128, T]); XK2 = di("xk2", [128, T])
    identbf = di("identbf", [128, 128], BF16)
    swap32 = di("swap32", [128, 128], BF16)
    identf = di("identf", [128, 128])
    onesbf = di("onesbf", [128, 1], BF16)
    onesfrow = di("onesfrow", [1, 128])
    onesbrow = di("onesbrow", [1, 128], BF16)
    e0row = di("e0row", [1, 128]); e1row = di("e1row", [1, 128])
    Esel = di("Esel", [128, 4, 8], BF16)
    Bsel = di("Bsel", [8, 512], BF16)
    maskAdd = di("maskAdd", [128, 8, 1024], BF16)
    maskEdge = di("maskEdge", [128, 2, 128], BF16)
    WQKV = [di(f"wqkv{l}", [128, 8, 1536]) for l in range(2)]
    WO = [di(f"wo{l}", [128, 4, 1024]) for l in range(2)]
    QSC = [di(f"qsc{l}", [128, 1]) for l in range(2)]
    KSC = [di(f"ksc{l}", [128, 1]) for l in range(2)]
    QSCS = [di(f"qscS{l}", [128, 1]) for l in range(2)]
    KSCS = [di(f"kscS{l}", [128, 1]) for l in range(2)]
    LNG = [di(f"lnG{l}", [128, 8]) for l in range(2)]
    LNB = [di(f"lnB{l}", [128, 8]) for l in range(2)]
    W1A = [di(f"w1a{l}", [KT, 128, 8, 128]) for l in range(2)]
    W1G = [di(f"w1g{l}", [KT, 128, 8, 128]) for l in range(2)]
    B1A = [di(f"b1a{l}", [128, KT]) for l in range(2)]
    B1G = [di(f"b1g{l}", [128, KT]) for l in range(2)]
    W2P = [di(f"w2p{l}", [KT, 128, 1024]) for l in range(2)]
    B2C = [di(f"b2c{l}", [128, 8]) for l in range(2)]

    dsc = lambda nm, sh, dt: nc.dram_tensor(nm, sh, dt, kind=dkind).ap()
    hX1 = dsc("hX1", [128, 8, T], F32)
    hY = dsc("hY", [128, 8, T], F32)
    hQh = dsc("hQh", [128, 8, T], BF16)
    hKh = dsc("hKh", [128, 8, T], BF16)
    hQi = dsc("hQi", [128, 4, T], BF16)
    hOi = dsc("hOi", [128, 4, T], BF16)

    PS0 = nc.alloc_psum_tensor("PS0", [128, 1024], F32).ap()
    PS1 = nc.alloc_psum_tensor("PS1", [128, 1024], F32).ap()
    PS2 = nc.alloc_psum_tensor("PS2", [128, 512], F32).ap()
    PS3 = nc.alloc_psum_tensor("PS3", [128, 512], F32).ap()
    PTR0 = nc.alloc_psum_tensor("PTR0", [128, 1024], BF16).ap()
    PTR1 = nc.alloc_psum_tensor("PTR1", [128, 1024], BF16).ap()
    PTR0f = PTR0.bitcast(F32)
    PTR1f = PTR1.bitcast(F32)
    PS0a, PS0b = PS0[:, 0:512], PS0[:, 512:1024]
    PS1a, PS1b = PS1[:, 0:512], PS1[:, 512:1024]

    ARENA_SZ = 204 * 1024
    _arena = nc.alloc_sbuf_tensor("arena", [128, ARENA_SZ], mybir.dt.uint8)
    _base = nc.lookup_mloc(_arena).addr
    A = lambda nm, sh, dt, off: nc.alloc_sbuf_tensor_at(nm, sh, dt,
                                                        offset=_base + off)
    co = 0
    def CA(nm, sh, dt):
        nonlocal co
        h = A(nm, sh, dt, co)
        per = int(np.prod(sh[1:])) * (2 if dt == BF16 else 4)
        co += (per + 31) // 32 * 32
        return h
    c_identbf = CA("c_identbf", [128, 128], BF16)
    c_swap32 = CA("c_swap32", [128, 128], BF16)
    c_identf = CA("c_identf", [128, 128], F32)
    c_onesbf = CA("c_onesbf", [128, 1], BF16)
    c_onesfrow = CA("c_onesfrow", [1, 128], F32)
    c_onesbrow = CA("c_onesbrow", [1, 128], BF16)
    c_e0 = CA("c_e0", [1, 128], F32)
    c_e1 = CA("c_e1", [1, 128], F32)
    c_Esel = CA("c_Esel", [128, 4, 8], BF16)
    c_Bsel = CA("c_Bsel", [8, 512], BF16)
    c_wqB = CA("c_wqB", [128, 8], F32)
    c_wkB = CA("c_wkB", [128, 8], F32)
    c_wvB = CA("c_wvB", [128, 8], F32)
    c_qsc = CA("c_qsc", [128, 1], F32)
    c_ksc = CA("c_ksc", [128, 1], F32)
    c_qscS = CA("c_qscS", [128, 1], F32)
    c_kscS = CA("c_kscS", [128, 1], F32)
    c_lnG = CA("c_lnG", [128, 8], F32)
    c_lnB = CA("c_lnB", [128, 8], F32)
    c_b1a = CA("c_b1a", [128, KT], F32)
    c_b1g = CA("c_b1g", [128, KT], F32)
    c_b2 = CA("c_b2", [128, 8], F32)
    c_pm2 = CA("c_pm2", [128, 512], BF16)
    assert co <= 8 * 1024, co

    FB = 8 * 1024
    fo = FB
    def FA(nm, sh, dt=F32):
        nonlocal fo
        h = A(nm, sh, dt, fo)
        per = int(np.prod(sh[1:])) * (2 if dt == BF16 else 4)
        fo += (per + 31) // 32 * 32
        return h
    r_mu = FA("r_mu", [1, 512]); r_m2 = FA("r_m2", [1, 512])
    r_rstd = FA("r_rstd", [1, 512])
    r_t0 = FA("r_t0", [8, 512]); r_t1 = FA("r_t1", [8, 512])
    s_B1 = FA("s_B1", [128, 512]); s_B2 = FA("s_B2", [128, 512])
    s_r8a = FA("s_r8a", [8, 512]); s_r8b = FA("s_r8b", [8, 512])
    s_r8c = FA("s_r8c", [8, 512])
    s_Brinv = FA("s_Brinv", [128, 512])
    s_cols = FA("s_cols", [128, 32])
    assert fo <= 32 * 1024, fo
    r_var = r_m2
    r_nmr = r_mu
    s_rrow = A("s_rrow", [1, 512], F32, 8 * 1024 + 2048 * 3)
    s_rrow2 = A("s_rrow2", [1, 512], F32, 8 * 1024 + 2048 * 4)
    r_hiB = A("r_hiB", [1, 512], BF16, 8 * 1024 + 2048 * 7)
    r_loB = A("r_loB", [1, 512], BF16, 8 * 1024 + 2048 * 7 + 1024)
    r_hiC = A("r_hiC", [1, 512], BF16, 8 * 1024 + 2048 * 8)
    r_loC = A("r_loC", [1, 512], BF16, 8 * 1024 + 2048 * 8 + 1024)

    BIG = 32 * 1024
    TOP = 204 * 1024
    p1_wq = A("p1_wq", [128, 8, 1024], BF16, BIG)
    p1_wk = A("p1_wk", [128, 8, 1024], BF16, BIG + 16384)
    p1_wv = A("p1_wv", [128, 8, 1024], BF16, BIG + 32768)
    VOFF = BIG + 49152
    p1_V = A("p1_V", [128, NT, 1024], BF16, VOFF)
    VSZ = NT * 1024 * 2
    o = VOFF + VSZ
    p1_xbf = [A(f"p1_xbf{i}", [128, 8, 512], BF16, o + i * 8192) for i in range(2)]
    o += 16384
    p1_sq = A("p1_sq", [128, 8, 512], BF16, o); o += 8192
    p1_lnx = [A(f"p1_lnx{i}", [128, 8, 512], BF16, o + i * 8192) for i in range(2)]
    o += 16384
    p1_rc = [A(f"p1_rc{i}", [128, 4, 512], BF16, o + i * 4096) for i in range(2)]
    o += 8192
    p1_rs = [A(f"p1_rs{i}", [128, 4, 512], BF16, o + i * 4096) for i in range(2)]
    o += 8192
    p1_xfd = [A("p1_xfdA", [128, 512], F32, FB + 14336),
              A("p1_xfdB", [128, 512], F32, FB + 16384)]
    p1_qe = A("p1_qe", [128, 2, 512], BF16, o); o += 2048
    p1_qo = A("p1_qo", [128, 2, 512], BF16, o); o += 2048
    p1_m1 = A("p1_m1", [128, 512], BF16, FB + 22656)
    p1_m2 = c_pm2
    p1_tmp = A("p1_tmp", [128, 512], F32, FB + 18432)
    p3_y = [A(f"p3_y{i}", [128, 512], F32, FB + 20480) for i in range(1)]
    assert o <= TOP, o

    p2_PT = A("p2_PT", [128, NT, 512], BF16, BIG)
    assert BIG + NT * 1024 <= VOFF
    p2_kc = [A(f"p2_kc{i}", [128, 8, 512], BF16, VOFF - 16384 + i * 8192)
             for i in range(2)]
    assert NT * 1024 <= 49152 - 16384
    o2 = VOFF + VSZ
    p2_S = [A(f"p2_S{i}", [128, T], BF16, o2 + i * T * 2) for i in range(4)]
    o2 += 4 * T * 2
    p2_P = [A(f"p2_P{i}", [128, T], BF16, o2 + i * T * 2) for i in range(2)]
    o2 += 2 * T * 2
    p2_qsl = A("p2_qsl", [128, 8, 512], BF16, o2); o2 += 8192
    p2_mp = A("p2_mp", [128, 4, 16], F32, o2); o2 += 256
    assert o2 <= TOP, o2
    p2_xs = [A("p2_xsA", [128, 512], F32, FB), A("p2_xsB", [128, 512], F32, FB + 2048)]
    p2_x1 = [A("p2_x1A", [128, 512], F32, FB + 4096),
             A("p2_x1B", [128, 512], F32, FB + 8192)]
    p2_tmp = A("p2_tmpX", [128, 512], F32, FB + 10240)

    o4 = BIG
    p4_Kp = A("p4_Kp", [128, 4, WIN + T], BF16, o4); o4 += 4 * (WIN + T) * 2
    p4_Vp = A("p4_Vp", [128, NT + 4, 512], BF16, o4); o4 += (NT + 4) * 1024
    p4_wo = A("p4_wo", [128, 4, 1024], BF16, o4); o4 += 8192
    moff = o4
    p4_msk = A("p4_msk", [128, 8, 1024], BF16, o4); o4 += 16384
    p4_me = A("p4_me", [128, 2, 128], BF16, o4); o4 += 512
    O4B = o4
    oq = O4B
    p4_wqkv = A("p4_wqkv", [128, 8, 1536], BF16, oq); oq += 24576
    p4_hs = [A(f"p4_hs{i}", [128, 8, 512], BF16, oq + i * 8192) for i in range(2)]
    oq += 16384
    p4_qc = [A(f"p4_qc{i}", [128, 4, 512], BF16, oq + i * 4096) for i in range(2)]
    oq += 8192
    p4_kc2 = [A(f"p4_kc2{i}", [128, 4, 512], BF16, oq + i * 4096) for i in range(2)]
    oq += 8192
    p4_sq = A("p4_sq", [128, 4, 512], BF16, oq); oq += 4096
    p4_tb = A("p4_tb", [128, 4, 512], BF16, oq); oq += 4096
    p4_tf = [A(f"p4_tf{i}", [128, 512], F32, oq + i * 2048) for i in range(2)]
    oq += 4096
    p4_brk = A("p4_brk", [128, 512], BF16, oq); oq += 1024
    p4_qh = A("p4_qh", [128, 512], BF16, oq); oq += 1024
    p4_m1 = A("p4_m1", [128, 512], BF16, oq); oq += 1024
    p4_m2 = A("p4_m2", [128, 512], BF16, oq); oq += 1024
    p4_rkb = A("p4_rkb", [8, 512], BF16, oq); oq += 1024
    p4_rqb = A("p4_rqb", [8, 512], BF16, oq); oq += 1024
    assert oq <= TOP, oq
    oa = O4B
    p4_PT = [A(f"p4_PT{i}", [128, 8, 512], BF16, oa + i * 8192) for i in range(2)]
    oa += 16384
    p4_P2 = [A(f"p4_P2{i}", [128, 1024], BF16, oa + i * 2048) for i in range(2)]
    oa += 4096
    p4_P2 += [A(f"p4_P2{i + 2}", [128, 1024], BF16, moff + i * 2048)
              for i in range(2)]
    p4_qi = [A(f"p4_qi{i}", [128, 4, 512], BF16, oa + i * 4096) for i in range(2)]
    oa += 8192
    p4_Og = [A(f"p4_Og{i}", [128, 4, 512], BF16, oa + i * 4096) for i in range(2)]
    oa += 8192
    p4_ysf = [A(f"p4_ysf{i}", [128, 8, 512], F32, oa + i * 16384) for i in range(2)]
    oa += 32768
    p4_y1 = [A(f"p4_y1N{i}", [128, 512], F32, oa + i * 2048) for i in range(2)]
    oa += 4096
    p4_mxb = A("p4_mxb", [128, 8], F32, oa); oa += 32
    p4_rsb = A("p4_rsb", [128, 8], F32, oa); oa += 32
    p4_rsi = A("p4_rsi", [128, 8], F32, oa); oa += 32
    p4_nb = A("p4_nb", [128, 8], F32, oa); oa += 32
    assert oa <= TOP

    o6 = BIG
    p6_At = [A(f"p6_At{i}", [128, 512], BF16, o6 + i * 1024) for i in range(2)]
    o6 += 2048
    p6_G = A("p6_G", [128, KT, CH6], BF16, o6); o6 += KT * CH6 * 2
    p6_W2 = A("p6_W2", [128, KT, 1024], BF16, o6); o6 += KT * 2048
    p6_hb = A("p6_hb", [128, 8, CH6], BF16, o6); o6 += 8 * CH6 * 2
    p6_ln = A("p6_ln", [128, 8, CH6], BF16, o6); o6 += 8 * CH6 * 2
    p6_sq = A("p6_sq", [128, 8, 512], BF16, o6); o6 += 8192
    p6_hf = [A(f"p6_hf{i}", [128, 512], F32, o6 + i * 2048) for i in range(2)]
    o6 += 4096
    p6_w1a = [A(f"p6_w1a{i}", [128, 8, 128], BF16, o6 + i * 2048) for i in range(2)]
    o6 += 4096
    p6_w1g = [A(f"p6_w1g{i}", [128, 8, 128], BF16, o6 + i * 2048) for i in range(2)]
    o6 += 4096
    p6_ys = [A(f"p6_ys{i}", [128, 512], F32, o6 + i * 2048) for i in range(2)]
    o6 += 4096
    p6_xs = [A(f"p6_xs{i}", [128, 512], F32, o6 + i * 2048) for i in range(2)]
    o6 += 4096
    p6_t1 = A("p6_t1", [128, 512], F32, o6); o6 += 2048
    p6_bnd = A("p6_bnd", [128, 4, 8], F32, o6); o6 += 128
    p6_ln2 = A("p6_ln2", [128, 8, CH6], BF16, o6); o6 += 8 * CH6 * 2
    p6_lnd = [p6_ln, p6_ln2]
    assert o6 <= TOP, o6

    for dst, src in ((c_identbf, identbf), (c_swap32, swap32),
                     (c_identf, identf), (c_onesbf, onesbf),
                     (c_onesfrow, onesfrow), (c_onesbrow, onesbrow),
                     (c_e0, e0row), (c_e1, e1row),
                     (c_Esel, Esel), (c_Bsel, Bsel), (c_wqB, wqB),
                     (c_wkB, wkB), (c_wvB, wvB)):
        K.dma("qsC", dst.ap(), src)
    for dst, src in ((p1_wq, wqT), (p1_wk, wkT), (p1_wv, wvT)):
        K.dma("qgS", dst.ap(), src)
    consts_done = K.cnt["qsC"]
    dep_w = ("qgS", K.cnt["qgS"])

    ln_evac = [0]

    def ln_stats(xbf_ap, sq_ap, dep_load, nfeat=8, csq=None, bdst=None,
                 bextra=()):
        if csq is None:
            csq = K.op("sc", "activation", sq_ap, xbf_ap, AF.Square,
                       deps=[dep_load, ("pe", K.cnt["pe"])])
        dep = [("sc", csq), ("qsC", K.cnt["qsC"]), ("dve", ln_evac[0])]
        if dep_load is not None:
            dep.append(dep_load)
        c1 = K.mmacc(PS2[0:1, :], [(c_onesbf.ap(), xbf_ap[:, dt, :])
                                   for dt in range(nfeat)], deps=dep)
        c2 = K.mmacc(PS3[0:1, :], [(c_onesbf.ap(), sq_ap[:, dt, :])
                                   for dt in range(nfeat)], deps=dep)
        nf = float(nfeat * 128)
        K.op("dve", "tensor_scalar_mul", r_mu.ap(), PS2[0:1, :], 1.0 / nf,
             deps=[("pe", c1)])
        ev2 = K.op("dve", "tensor_scalar_mul", r_m2.ap(), PS3[0:1, :], 1.0 / nf,
                   deps=[("pe", c2)])
        t0 = r_t0.ap()[0:1, :]
        t1 = r_t1.ap()[0:1, :]
        K.op("dve", "tensor_mul", t0, r_mu.ap(), r_mu.ap())
        K.op("dve", "tensor_sub", r_var.ap(), r_m2.ap(), t0)
        cv = K.op("dve", "tensor_scalar_add", r_var.ap(), r_var.ap(), 1e-5)
        cs = K.op("sc", "activation", t0, r_var.ap(), AF.Sqrt,
                  deps=[("dve", cv)])
        K.op("dve", "reciprocal", r_rstd.ap(), t0, deps=[("sc", cs)])
        K.op("dve", "tensor_mul", t0, r_rstd.ap(), r_rstd.ap())
        K.op("dve", "tensor_mul", t1, r_var.ap(), t0)
        K.op("dve", "tensor_scalar", t1, t1, -0.5, 1.5,
             ALU.mult, ALU.add)
        K.op("dve", "tensor_mul", r_rstd.ap(), r_rstd.ap(), t1)
        K.op("dve", "tensor_mul", r_nmr.ap(), r_mu.ap(), r_rstd.ap())
        K.op("dve", "tensor_scalar_mul", r_nmr.ap(), r_nmr.ap(), -1.0)
        K.op("dve", "tensor_copy", r_hiB.ap(), r_rstd.ap())
        K.op("dve", "tensor_sub", t0, r_rstd.ap(), r_hiB.ap())
        cl1 = K.op("dve", "tensor_copy", r_loB.ap(), t0)
        K.op("dve", "tensor_copy", r_hiC.ap(), r_nmr.ap())
        K.op("dve", "tensor_sub", t1, r_nmr.ap(), r_hiC.ap())
        cl2 = K.op("dve", "tensor_copy", r_loC.ap(), t1)
        if bdst is not None:
            cb1 = K.mmacc(bdst[0], [(c_onesbrow.ap(), r_hiB.ap()),
                                    (c_onesbrow.ap(), r_loB.ap())],
                          deps=[("dve", cl2), ("qsC", K.cnt["qsC"])]
                               + list(bextra))
            cb2 = K.mmacc(bdst[1], [(c_onesbrow.ap(), r_hiC.ap()),
                                    (c_onesbrow.ap(), r_loC.ap())],
                          deps=[])
            ln_evac[0] = ev2
            return ("pe", cb2)
        cb1 = K.mmacc(PS2, [(c_onesbrow.ap(), r_hiB.ap()),
                            (c_onesbrow.ap(), r_loB.ap())],
                      deps=[("dve", cl1), ("qsC", K.cnt["qsC"])])
        cb2 = K.mmacc(PS3, [(c_onesbrow.ap(), r_hiC.ap()),
                            (c_onesbrow.ap(), r_loC.ap())],
                      deps=[("dve", cl2)])
        K.op("dve", "tensor_copy", s_B1.ap(), PS2, deps=[("pe", cb1)])
        cvf = K.op("dve", "tensor_copy", s_B2.ap(), PS3, deps=[("pe", cb2)])
        ln_evac[0] = cvf
        return ("dve", cvf)

    ev4 = [0, 0, 0, 0]
    qe_free = [0]
    psn = [0]
    qo_war = [("qsS", 0)]
    stat_pe = [0]
    sq_hold = {}
    rope_done = [(0, 0), (0, 0)]

    def p01_part1(c):
        c0 = c * 512
        hb = c % 2
        lb_ = K.dmaW(c, p1_xbf[hb].ap(), xT[:, :, c0:c0 + 512],
                     deps=[("sc", K.cnt["sc"]), ("pe", stat_pe[0])])
        K.dmaW(c, p1_rc[hb].ap(), ropeC[:, :, c0:c0 + 512],
               deps=[("dve", rope_done[hb][0])])
        K.dmaW(c, p1_rs[hb].ap(), ropeS[:, :, c0:c0 + 512])
        sq_hold[c] = K.op("sc", "activation", p1_sq.ap(), p1_xbf[hb].ap(),
                          AF.Square, deps=[lb_, ("pe", stat_pe[0])])

    def p01_part2(c):
        c0 = c * 512
        hb = c % 2
        cb = ln_stats(p1_xbf[hb].ap(), p1_sq.ap(), None, csq=sq_hold[c])
        stat_pe[0] = K.cnt["pe"]
        assert cb[0] == "dve"
        lnx = p1_lnx[hb]
        cl = None
        for dt in range(8):
            lf = K.dmaL(dt, p1_xfd[dt % 2].ap(), xT[:, dt, c0:c0 + 512],
                        deps=[("dve", K.cnt["dve"])])
            K.op("dve", "tensor_mul", p1_tmp.ap(), p1_xfd[dt % 2].ap(),
                 s_B1.ap(), deps=[lf, cb])
            cl = K.op("dve", "tensor_add", lnx.ap()[:, dt, :], p1_tmp.ap(),
                      s_B2.ap())
        return ("dve", cl)

    def p01_s2(c, dep_lnx):
        c0 = c * 512
        hb = c % 2
        lnx = p1_lnx[hb]
        bank = [PS0a, PS0b, PS1a, PS1b]
        bi = 0
        for tt2 in range(4):
            for eh in range(2):
                ps = bank[bi % 4]
                cm = K.mmacc(ps, [(lnx.ap()[:, dt, tt2 * 128:(tt2 + 1) * 128],
                                   p1_wv.ap()[:, dt, eh * 512:(eh + 1) * 512])
                                  for dt in range(8)],
                             deps=[dep_lnx, dep_w, ("sc", ev4[bi % 4])])
                ev4[bi % 4] = K.op("sc", "activation",
                                   p1_V.ap()[:, c * 4 + tt2, eh * 512:(eh + 1) * 512],
                                   ps, AF.Copy, deps=[("pe", cm)])
                bi += 1
        cv = 0
        for (wsb, bias, hdst) in ((p1_wq, c_wqB, hQh), (p1_wk, c_wkB, hKh)):
            for pair in range(4):
                blo, bhi = bi % 4, (bi + 1) % 4
                clo = K.mmacc(bank[blo], [(wsb.ap()[:, dt, pair * 128:(pair + 1) * 128],
                                           lnx.ap()[:, dt, :]) for dt in range(8)],
                              deps=[dep_lnx, dep_w, ("sc", ev4[blo])])
                chi = K.mmacc(bank[bhi], [(wsb.ap()[:, dt, (pair + 4) * 128:(pair + 5) * 128],
                                           lnx.ap()[:, dt, :]) for dt in range(8)],
                              deps=[("sc", ev4[bhi])])
                ce1 = K.op("sc", "activation", p1_qe.ap()[:, 0, :], bank[blo],
                           AF.Identity, bias=bias.ap()[:, pair:pair + 1],
                           deps=[("pe", clo), ("qsC", consts_done),
                                 ("dve", qe_free[0])])
                ev4[blo] = ce1
                ce2 = K.op("sc", "activation", p1_qe.ap()[:, 1, :], bank[bhi],
                           AF.Identity, bias=bias.ap()[:, pair + 4:pair + 5],
                           deps=[("pe", chi)])
                ev4[bhi] = ce2
                bi += 2
                ct = p1_rc[hb].ap()[:, pair, :]
                st = p1_rs[hb].ap()[:, pair, :]
                sch = "qsS" if psn[0] % 2 == 0 else "qsS2"
                psn[0] += 1
                K.op("dve", "tensor_mul", p1_m1.ap(), p1_qe.ap()[:, 0, :], ct,
                     deps=[("sc", ce2), ("qgW", K.cnt["qgW"]),
                           ("qgW2", K.cnt["qgW2"])])
                K.op("dve", "tensor_mul", p1_m2.ap(), p1_qe.ap()[:, 1, :], st)
                csub = K.op("dve", "tensor_sub", p1_qo.ap()[:, 0, :],
                            p1_m1.ap(), p1_m2.ap(), deps=[qo_war[0]])
                K.op("dve", "tensor_mul", p1_m1.ap(), p1_qe.ap()[:, 0, :], st)
                qe_free[0] = K.op("dve", "tensor_mul", p1_m2.ap(),
                                  p1_qe.ap()[:, 1, :], ct)
                cv = K.op("dve", "tensor_add", p1_qo.ap()[:, 1, :], p1_m1.ap(),
                          p1_m2.ap(), deps=[qo_war[0]])
                K.dma(sch, hdst[:, pair, c0:c0 + 512], p1_qo.ap()[:, 0, :],
                      deps=[("dve", csub), (sch, K.cnt[sch])])
                K.dma(sch, hdst[:, pair + 4, c0:c0 + 512], p1_qo.ap()[:, 1, :],
                      deps=[("dve", cv)])
                qo_war[0] = (sch, K.cnt[sch])
        rope_done[hb] = (cv, K.cnt["qsS"])

    p01_part1(0)
    dep_lnx = p01_part2(0)
    p01_part1(1)
    for c in range(NCH):
        dep_nxt = p01_part2(c + 1) if c + 1 < NCH else None
        p01_s2(c, dep_lnx)
        if c + 2 < NCH:
            p01_part1(c + 2)
        dep_lnx = dep_nxt
    K.barrier()

    inv_sd = 1.0 / 32.0
    ev_dep = [0, 0, 0, 0]
    mx_dep = [0, 0, 0, 0]
    exp_cnt2 = [0, 0, 0, 0]
    p_free2 = [0, 0]
    pv_done = K.cnt["pe"]
    ptr_ev2 = [(0, 0), (0, 0)]
    for s in range(NSUP):
        s0 = s * 512
        lq = K.dmaL(s, p2_qsl.ap(), hQh[:, :, s0:s0 + 512],
                    deps=[("pe", K.cnt["pe"])])
        for jj in range(NSUP):
            jb = jj % 2
            lk = K.dmaL(jj, p2_kc[jb].ap(), hKh[:, :, jj * 512:jj * 512 + 512],
                        deps=[("pe", K.cnt["pe"])])
            for itl in range(4):
                ps = (PS0a, PS0b, PS1a, PS1b)[itl]
                cm = K.mmacc(ps, [(p2_qsl.ap()[:, dt, itl * 128:(itl + 1) * 128],
                                   p2_kc[jb].ap()[:, dt, :]) for dt in range(8)],
                             deps=[lk, lq, ("sc", ev_dep[itl]),
                                   ("pe", pv_done)])
                ev_dep[itl] = K.op("sc", "activation",
                                   p2_S[itl].ap()[:, jj * 512:jj * 512 + 512],
                                   ps, AF.Copy, deps=[("pe", cm)])
                mx_dep[itl] = K.op("dve", "reduce_max",
                                   p2_mp.ap()[:, itl, jj:jj + 1],
                                   p2_S[itl].ap()[:, jj * 512:jj * 512 + 512],
                                   axis=mybir.AxisListType.X,
                                   deps=[("sc", ev_dep[itl])])

        def p2_tail(info):
            itl, pb, ce = info
            for w in range((NT + 7) // 8):
                nu = min(8, NT - w * 8)
                tp_ = w % 2
                ptr = PTR0 if tp_ == 0 else PTR1
                ctr = K.trpack(
                    [(ptr[:, u * 128:(u + 1) * 128],
                      p2_P[pb].ap()[:, (w * 8 + u) * 128:(w * 8 + u + 1) * 128])
                     for u in range(nu)], c_identbf.ap(),
                    deps=[("sc", ce), ("dve", ptr_ev2[tp_][0]),
                          ("sc", ptr_ev2[tp_][1]), ("pe", pv_done)])
                p_free2[pb] = ctr
                dst = p2_PT.ap()[:, w * 8:w * 8 + nu, itl * 128:(itl + 1) * 128]
                srcv = ptr[:, 0:nu * 128].rearrange("p (u n) -> p u n", u=nu)
                if tp_ == 0:
                    cev = K.op("dve", "tensor_copy", dst, srcv, deps=[("pe", ctr)])
                    ptr_ev2[0] = (cev, ptr_ev2[0][1])
                else:
                    cev = K.op("sc", "activation", dst, srcv, AF.Copy,
                               deps=[("pe", ctr)])
                    ptr_ev2[1] = (ptr_ev2[1][0], cev)

        prev = None
        for itl in range(4):
            K.op("dve", "reduce_max", s_cols.ap()[:, itl:itl + 1],
                 p2_mp.ap()[:, itl, 0:NSUP], axis=mybir.AxisListType.X,
                 deps=[("dve", mx_dep[itl]), ("sc", exp_cnt2[itl])])
            cv = K.op("dve", "tensor_scalar_mul", s_cols.ap()[:, 8 + itl:9 + itl],
                      s_cols.ap()[:, itl:itl + 1], -inv_sd)
            pb = itl % 2
            ce = K.op("sc", "activation", p2_P[pb].ap(), p2_S[itl].ap(), AF.Exp,
                      bias=s_cols.ap()[:, 8 + itl:9 + itl], scale=inv_sd,
                      accum_out=s_cols.ap()[:, 16 + itl:17 + itl],
                      deps=[("dve", cv), ("pe", p_free2[pb]),
                            ("sc", ev_dep[itl])])
            exp_cnt2[itl] = ce
            if prev is not None:
                p2_tail(prev)
            prev = (itl, pb, ce)
        p2_tail(prev)
        cv = K.op("dve", "reciprocal", s_cols.ap()[:, 24:28], s_cols.ap()[:, 16:20],
                  deps=[("sc", K.cnt["sc"])])
        ctr = K.trpack([(PS2[0:1, it * 128:(it + 1) * 128],
                         s_cols.ap()[:, 24 + it:25 + it]) for it in range(4)],
                       c_identf.ap(), deps=[("dve", cv), ("pe", pv_done)])
        cv = K.op("dve", "tensor_copy", s_rrow.ap(), PS2[0:1, :], deps=[("pe", ctr)])
        K.op("dve", "tensor_copy", r_hiB.ap(), s_rrow.ap())
        K.op("dve", "tensor_sub", r_t0.ap()[0:1, :], s_rrow.ap(), r_hiB.ap())
        cl = K.op("dve", "tensor_copy", r_loB.ap(), r_t0.ap()[0:1, :])
        cb = K.mmacc(PS3, [(c_onesbrow.ap(), r_hiB.ap()),
                           (c_onesbrow.ap(), r_loB.ap())], deps=[("dve", cl)])
        cbr = K.op("dve", "tensor_copy", s_Brinv.ap(), PS3, deps=[("pe", cb)])
        for ec in range(8):
            xb_ = ec % 2
            lx = K.dmaL(ec, p2_xs[xb_].ap(), xT[:, ec, s0:s0 + 512],
                        deps=[("dve", K.cnt["dve"])])
            ps = PS2 if ec % 2 == 0 else PS3
            cm = K.mmacc(ps, [(p1_V.ap()[:, jt, ec * 128:(ec + 1) * 128],
                               p2_PT.ap()[:, jt, :]) for jt in range(NT)],
                         deps=[("dve", K.cnt["dve"]), ("sc", K.cnt["sc"])])
            cv = K.op("dve", "tensor_mul", p2_tmp.ap(), ps, s_Brinv.ap(),
                      deps=[("pe", cm), ("dve", cbr)])
            cv = K.op("dve", "scalar_tensor_tensor", p2_x1[xb_].ap(), p2_tmp.ap(),
                      c_wvB.ap()[:, ec:ec + 1], p2_xs[xb_].ap(), ALU.add, ALU.add,
                      deps=[lx, ("qsS", K.cnt["qsS"])])
            K.dma("qsS", hX1[:, ec, s0:s0 + 512], p2_x1[xb_].ap(), deps=[("dve", cv)])
        pv_done = K.cnt["pe"]
    K.barrier()

    lb_ = K.dmaW(0, p1_xbf[0].ap(), hX1[:, :, 0:512],
                 deps=[("sc", K.cnt["sc"]), ("pe", K.cnt["pe"])])
    for c in range(NCH):
        c0 = c * 512
        cb = ln_stats(p1_xbf[c % 2].ap(), p1_sq.ap(), lb_)
        if c + 1 < NCH:
            lb_ = K.dmaW(c + 1, p1_xbf[(c + 1) % 2].ap(),
                         hX1[:, :, c0 + 512:c0 + 1024],
                         deps=[("sc", K.cnt["sc"]), ("pe", K.cnt["pe"])])
        for dt in range(8):
            lf = K.dmaL(dt, p1_xfd[dt % 2].ap(), hX1[:, dt, c0:c0 + 512],
                        deps=[("dve", K.cnt["dve"])])
            K.op("dve", "tensor_mul", p1_tmp.ap(), p1_xfd[dt % 2].ap(), s_B1.ap(),
                 deps=[lf, cb])
            cv = K.op("dve", "tensor_add", p3_y[0].ap(), p1_tmp.ap(), s_B2.ap(),
                      deps=[("qsS", K.cnt["qsS"])])
            K.dma("qsS", hY[:, dt, c0:c0 + 512], p3_y[0].ap(), deps=[("dve", cv)])
    K.barrier()

    for l in range(nlayers):
        K.dma("qsC", c_qsc.ap(), QSC[l], deps=[("dve", K.cnt["dve"])])
        K.dma("qsC", c_ksc.ap(), KSC[l], deps=[])
        K.dma("qsC", c_qscS.ap(), QSCS[l]); K.dma("qsC", c_kscS.ap(), KSCS[l])
        K.dma("qsC", p4_msk.ap()[:, 4:8, :], maskAdd[:, 4:8, :],
              deps=[("dve", K.cnt["dve"]),
                                                 ("sc", K.cnt["sc"]),
                                                 ("pe", K.cnt["pe"])])
        K.dma("qsC", p4_me.ap(), maskEdge)
        K.dma("qsC", c_lnG.ap(), LNG[l]); K.dma("qsC", c_lnB.ap(), LNB[l])
        K.dma("qsC", c_b1a.ap(), B1A[l]); K.dma("qsC", c_b1g.ap(), B1G[l])
        cpl = K.dma("qsC", c_b2.ap(), B2C[l])
        K.dma("qgS", p4_wqkv.ap(), WQKV[l], deps=[("pe", K.cnt["pe"])])
        K.dma("qgS", p4_wo.ap(), WO[l])
        cz1 = K.op("gp", "memset", p4_Kp.ap()[:, :, 0:WIN], 0.0,
                   deps=[("pe", K.cnt["pe"])])
        cz2 = K.op("gp", "memset", p4_Vp.ap()[:, 0:4, :], 0.0)

        qk_done = [(0, 0, 0)] * 2
        esel_pe = [K.cnt["pe"]]

        hload = {}

        def p4a_load(c):
            c0 = c * 512
            hb = c % 2
            hs = p4_hs[hb]
            dep_pe = ("pe", K.cnt["pe"])
            K.dmaW(c, hs.ap()[:, 0:4, :], hY[:, 0:4, c0:c0 + 512], deps=[dep_pe])
            if c0 == 0:
                K.op("gp", "memset", hs.ap()[:, 4:8, 0:1], 0.0, deps=[dep_pe])
                l2 = K.dmaW(c, hs.ap()[:, 4:8, 1:512], hY[:, 4:8, 0:511])
            else:
                l2 = K.dmaW(c, hs.ap()[:, 4:8, :], hY[:, 4:8, c0 - 1:c0 + 511])
            hload[c] = (l2, ("gp", K.cnt["gp"]))

        def p4a_s1(c):
            c0 = c * 512
            hb = c % 2
            hs = p4_hs[hb]
            dep_h = list(hload[c])
            scq, dvq, pesw = qk_done[hb]
            for qk, dst in ((0, p4_qc[hb]), (1, p4_kc2[hb])):
                for tile in range(4):
                    ecol = qk * 512 + tile * 128
                    ps = PS2 if tile % 2 == 0 else PS3
                    cm = K.mmacc(ps, [(p4_wqkv.ap()[:, dt, ecol:ecol + 128],
                                       hs.ap()[:, dt, :]) for dt in range(8)],
                                 deps=dep_h + [("qgS", K.cnt["qgS"]),
                                               ("sc", K.cnt["sc"]),
                                               ("dve", K.cnt["dve"])])
                    K.op("sc", "activation", dst.ap()[:, tile, :], ps, AF.Copy,
                         deps=[("pe", cm), ("sc", scq), ("dve", dvq),
                               ("pe", pesw)])
            for tt2 in range(4):
                ps = PS2 if tt2 % 2 == 0 else PS3
                cm = K.mmacc(ps, [(hs.ap()[:, dt, tt2 * 128:(tt2 + 1) * 128],
                                   p4_wqkv.ap()[:, dt, 1024:1536])
                                  for dt in range(8)],
                             deps=[("sc", K.cnt["sc"]), ("dve", K.cnt["dve"])])
                K.op("sc", "activation", p4_Vp.ap()[:, 4 + c * 4 + tt2, :], ps,
                     AF.Copy, deps=[("pe", cm)])

        def p4a_s2(c):
            c0 = c * 512
            hb = c % 2
            for src_, rowdst in ((p4_qc[hb], s_r8a), (p4_kc2[hb], s_r8b)):
                csq = K.op("sc", "activation", p4_sq.ap(), src_.ap(), AF.Square,
                           deps=[("pe", esel_pe[0])])
                cm = K.mmacc(PS2[0:8, :], [(c_Esel.ap()[:, tile, :],
                                            p4_sq.ap()[:, tile, :])
                                           for tile in range(4)],
                             deps=[("sc", csq), ("qsC", consts_done),
                                   ("dve", K.cnt["dve"])])
                esel_pe[0] = cm
                K.op("dve", "tensor_scalar_max", s_r8c.ap(), PS2[0:8, :], 1e-24,
                     deps=[("pe", cm)])
                cs = K.op("sc", "activation", rowdst.ap(), s_r8c.ap(), AF.Sqrt,
                          deps=[("dve", K.cnt["dve"])])
                K.op("dve", "reciprocal", rowdst.ap(), rowdst.ap(), deps=[("sc", cs)])
                K.op("dve", "tensor_mul", r_t0.ap()[0:8, :], rowdst.ap(), rowdst.ap())
                K.op("dve", "tensor_mul", r_t1.ap()[0:8, :], s_r8c.ap(),
                     r_t0.ap()[0:8, :])
                K.op("dve", "tensor_scalar", r_t1.ap()[0:8, :], r_t1.ap()[0:8, :],
                     -0.5, 1.5, ALU.mult, ALU.add)
                K.op("dve", "tensor_mul", rowdst.ap(), rowdst.ap(), r_t1.ap()[0:8, :])
            K.op("dve", "tensor_scalar_mul", s_r8b.ap(), s_r8b.ap(), 8.0)
            cv = K.op("dve", "tensor_copy", p4_rkb.ap(), s_r8b.ap())
            cvq = K.op("dve", "tensor_copy", p4_rqb.ap(), s_r8a.ap())
            for ti, (tab, scl) in enumerate(((XQ1, c_qsc), (XQ2, c_qscS),
                                             (XK1, c_ksc), (XK2, c_kscS))):
                lt = K.dmaL(ti, p4_tf[ti % 2].ap(), tab[:, c0:c0 + 512],
                            deps=[("dve", K.cnt["dve"])])
                K.op("dve", "tensor_scalar_mul", p4_tb.ap()[:, ti, :],
                     p4_tf[ti % 2].ap(), scl.ap(),
                     deps=[lt, ("qsC", K.cnt["qsC"])])
            last_sw = 0
            for tile in range(4):
                csw = K.mmacc(PS3, [(c_swap32.ap(), p4_qc[hb].ap()[:, tile, :])],
                              deps=[("dve", K.cnt["dve"]), ("sc", K.cnt["sc"]),
                                    ("qsC", K.cnt["qsC"])])
                K.op("dve", "tensor_mul", p4_m1.ap(), p4_qc[hb].ap()[:, tile, :],
                     p4_tb.ap()[:, 0, :])
                K.op("dve", "tensor_mul", p4_m2.ap(), PS3, p4_tb.ap()[:, 1, :],
                     deps=[("pe", csw)])
                cmq = K.mmacc(PS2, [(c_Bsel.ap()[:, tile * 128:(tile + 1) * 128],
                                     p4_rqb.ap())],
                              deps=[("dve", K.cnt["dve"])])
                cbq = K.op("dve", "tensor_copy", p4_brk.ap(), PS2, deps=[("pe", cmq)])
                K.op("dve", "tensor_add", p4_m1.ap(), p4_m1.ap(), p4_m2.ap(),
                     deps=[("qsS", K.cnt["qsS"])])
                cv = K.op("dve", "tensor_mul", p4_qh.ap(), p4_m1.ap(), p4_brk.ap())
                K.dma("qsS", hQi[:, tile, c0:c0 + 512], p4_qh.ap(), deps=[("dve", cv)])
                cm = K.mmacc(PS2, [(c_Bsel.ap()[:, tile * 128:(tile + 1) * 128],
                                    p4_rkb.ap())],
                             deps=[("dve", K.cnt["dve"])])
                cbk = K.op("dve", "tensor_copy", p4_brk.ap(), PS2, deps=[("pe", cm)])
                csw = K.mmacc(PS3, [(c_swap32.ap(), p4_kc2[hb].ap()[:, tile, :])],
                              deps=[("dve", K.cnt["dve"])])
                last_sw = csw
                K.op("dve", "tensor_mul", p4_m1.ap(), p4_kc2[hb].ap()[:, tile, :],
                     p4_tb.ap()[:, 2, :])
                K.op("dve", "tensor_mul", p4_m2.ap(), PS3, p4_tb.ap()[:, 3, :],
                     deps=[("pe", csw)])
                K.op("dve", "tensor_add", p4_m1.ap(), p4_m1.ap(), p4_m2.ap())
                cvl = K.op("dve", "tensor_mul",
                           p4_Kp.ap()[:, tile, WIN + c0:WIN + c0 + 512],
                           p4_m1.ap(), p4_brk.ap(), deps=[("gp", cz1)])
            qk_done[hb] = (K.cnt["sc"], K.cnt["dve"], last_sw)

        p4a_load(0)
        if NCH > 1:
            p4a_load(1)
        p4a_s1(0)
        for c in range(NCH):
            if c + 1 < NCH:
                p4a_s1(c + 1)
            if c + 2 < NCH:
                p4a_load(c + 2)
            p4a_s2(c)
        K.barrier()

        def p4_wo_part(sprev, dts):
            sp0 = sprev * 512
            ob = sprev % 2
            for dt in dts:
                yb_ = dt % 2
                ps = PS2 if dt % 2 == 0 else PS3
                cm = K.mmacc(ps, [(p4_wo.ap()[:, k, dt * 128:(dt + 1) * 128],
                                   p4_Og[ob].ap()[:, k, :]) for k in range(4)],
                             deps=[("qgS", K.cnt["qgS"]),
                                   ("dve", K.cnt["dve"]), ("sc", K.cnt["sc"])])
                cv = K.op("dve", "tensor_add", p4_y1[yb_].ap(), ps,
                          p4_ysf[ob].ap()[:, dt, :],
                          deps=[("pe", cm), ("qsL", K.cnt["qsL"]),
                                ("qsL2", K.cnt["qsL2"]),
                                ("qsS", K.cnt["qsS"])])
                K.dma("qsS", hY[:, dt, sp0 + 0:sp0 + 512] if False else
                      hY[:, dt, sp0:sp0 + 512], p4_y1[yb_].ap(),
                      deps=[("dve", cv)])

        for s in range(NSUP):
            s0 = s * 512
            qb = s % 2
            lq = K.dmaL(s, p4_qi[qb].ap(), hQi[:, :, s0:s0 + 512],
                        deps=[("pe", K.cnt["pe"])])
            K.dmaL(s, p4_ysf[s % 2].ap(), hY[:, :, s0:s0 + 512],
                   deps=[("dve", K.cnt["dve"])])
            psh_sc = [0, 0]
            psh_dve = [0, 0]
            p_free = [0, 0, 0, 0]
            ptr_ev = [(0, 0), (0, 0)]
            for hp in range(4):
                pt_dep = K.cnt["pe"]
                prev = None

                def emit_pair(it2):
                    bl = 0 if s == 0 else it2 * 128
                    b0 = s0 + bl
                    n2 = 512 if s == 0 else 128
                    K.wait("pe", [lq, ("sc", psh_sc[0]), ("dve", psh_dve[0]),
                                  ("sc", psh_sc[1]), ("dve", psh_dve[1]),
                                  ("pe", K.cnt["pe"])])
                    for c0_, n_ in ((0, 512), (512, n2)):
                        for hh in range(2):
                            PSH = PS0 if hh == 0 else PS1
                            ins = kb.e["pe"].matmul(
                                PSH[:, c0_:c0_ + n_],
                                p4_qi[qb].ap()[64 * hh:64 * hh + 64, hp,
                                               it2 * 128:(it2 + 1) * 128],
                                p4_Kp.ap()[64 * hh:64 * hh + 64, hp,
                                           b0 + c0_:b0 + c0_ + n_],
                                start=True, stop=True,
                                tile_position=(64 * hh, 0))
                            kb.cnt["pe"] += 1
                            ins.then_inc(kb.sem["pe"], 1)
                    return kb.cnt["pe"]

                def emit_scores(it2, hh, cm2):
                    h = 2 * hp + hh
                    PSH = PS0 if hh == 0 else PS1
                    hcol = hh * 4 + it2
                    bl = 0 if s == 0 else it2 * 128
                    bw = 1024 if s == 0 else 640
                    if s == 0:
                        K.op("dve", "tensor_add", PSH, PSH,
                             p4_msk.ap()[:, 4 + it2, :], deps=[("pe", cm2)])
                    else:
                        K.op("dve", "tensor_tensor",
                             PSH.rearrange("p (u n) -> p u n", u=8)[:, 0:8:4, :],
                             PSH.rearrange("p (u n) -> p u n", u=8)[:, 0:8:4, :],
                             p4_me.ap(), ALU.add, deps=[("pe", cm2)])
                    cnb = K.op("dve", "reduce_max",
                               p4_nb.ap()[:, hcol:hcol + 1], PSH[:, 0:bw],
                               axis=mybir.AxisListType.X, negate=True)
                    psh_dve[hh] = cnb
                    pb = (it2 * 2 + hh) % 4
                    ce = K.op("sc", "activation",
                              p4_P2[pb].ap()[:, bl:bl + bw], PSH[:, 0:bw],
                              AF.Exp, bias=p4_nb.ap()[:, hcol:hcol + 1],
                              accum_out=p4_rsb.ap()[:, hcol:hcol + 1],
                              deps=[("dve", cnb), ("pe", p_free[pb])])
                    psh_sc[hh] = ce
                    return (it2, hh, pb, ce)

                def emit_tail(info):
                    it2, hh, pb, ce = info
                    tp_ = (it2 * 2 + hh) % 2
                    ptr = PTR0 if tp_ == 0 else PTR1
                    jt0 = 0 if s == 0 else it2
                    njt = 8 if s == 0 else 5
                    ctr = K.trpack(
                        [(ptr[:, u * 128:(u + 1) * 128],
                          p4_P2[pb].ap()[:, (jt0 + u) * 128:(jt0 + u + 1) * 128])
                         for u in range(njt)],
                        c_identbf.ap(),
                        deps=[("sc", ce), ("dve", ptr_ev[tp_][0]),
                              ("sc", ptr_ev[tp_][1]), ("pe", pt_dep)])
                    p_free[pb] = ctr
                    dst = p4_PT[hh].ap()[:, jt0:jt0 + njt,
                                         it2 * 128:(it2 + 1) * 128]
                    srcv = ptr[:, 0:njt * 128].rearrange("p (u n) -> p u n", u=njt)
                    if tp_ == 0:
                        cev = K.op("dve", "tensor_copy", dst, srcv,
                                   deps=[("pe", ctr)])
                        ptr_ev[0] = (cev, ptr_ev[0][1])
                    else:
                        cev = K.op("sc", "activation", dst, srcv, AF.Copy,
                                   deps=[("pe", ctr)])
                        ptr_ev[1] = (ptr_ev[1][0], cev)
                    if s != 0:
                        if it2 > 0:
                            K.op("gp", "memset",
                                 p4_PT[hh].ap()[:, 0:it2,
                                                it2 * 128:(it2 + 1) * 128],
                                 0.0, deps=[("pe", pt_dep)])
                        if it2 + 5 < 8:
                            K.op("gp", "memset",
                                 p4_PT[hh].ap()[:, it2 + 5:8,
                                                it2 * 128:(it2 + 1) * 128],
                                 0.0, deps=[("pe", pt_dep)])

                pend2 = []
                for it2 in range(4):
                    cm2 = emit_pair(it2)
                    pend2.append(emit_scores(it2, 0, cm2))
                    pend2.append(emit_scores(it2, 1, cm2))
                    if it2 > 0:
                        emit_tail(pend2.pop(0))
                        emit_tail(pend2.pop(0))
                for info in pend2:
                    emit_tail(info)
                cv = K.op("dve", "reciprocal", p4_rsi.ap(), p4_rsb.ap(),
                          deps=[("sc", K.cnt["sc"])])
                ctr = K.trpack([(PS2[0:1, it * 128:(it + 1) * 128],
                                 p4_rsi.ap()[:, it:it + 1]) for it in range(4)],
                               c_identf.ap(), deps=[("dve", cv)])
                cv = K.op("dve", "tensor_copy", s_rrow.ap(), PS2[0:1, :],
                          deps=[("pe", ctr)])
                ctr = K.trpack([(PS2[0:1, it * 128:(it + 1) * 128],
                                 p4_rsi.ap()[:, 4 + it:5 + it]) for it in range(4)],
                               c_identf.ap(), deps=[("dve", cv)])
                cv2 = K.op("dve", "tensor_copy", s_rrow2.ap(), PS2[0:1, :],
                           deps=[("pe", ctr)])
                cb = K.mmacc(PS3, [(c_e0.ap(), s_rrow.ap()),
                                   (c_e1.ap(), s_rrow2.ap())],
                             deps=[("dve", cv2)])
                cbr = K.op("dve", "tensor_copy", s_Brinv.ap(), PS3, deps=[("pe", cb)])
                K.wait("pe", [("dve", K.cnt["dve"]), ("sc", K.cnt["sc"]),
                              ("gp", K.cnt["gp"]), ("pe", K.cnt["pe"])])
                ins = None
                for jt in range(8):
                    for hh in range(2):
                        PSV = PS2 if hh == 0 else PS3
                        ins = kb.e["pe"].matmul(
                            PSV[64 * hh:64 * hh + 64, :],
                            p4_Vp.ap()[:, s * 4 + jt,
                                       (2 * hp + hh) * 64:(2 * hp + hh) * 64 + 64],
                            p4_PT[hh].ap()[:, jt, :],
                            start=(jt == 0), stop=(jt == 7),
                            tile_position=(0, 64 * hh))
                kb.cnt["pe"] += 1
                ins.then_inc(kb.sem["pe"], 1)
                cpv = kb.cnt["pe"]
                K.op("dve", "tensor_mul", p4_Og[s % 2].ap()[0:64, hp, :],
                     PS2[0:64, :], s_Brinv.ap()[0:64, :],
                     deps=[("pe", cpv), ("dve", cbr)])
                K.op("dve", "tensor_mul", p4_Og[s % 2].ap()[64:128, hp, :],
                     PS3[64:128, :], s_Brinv.ap()[64:128, :],
                     deps=[("pe", cpv)])
                if s > 0:
                    p4_wo_part(s - 1, (2 * hp, 2 * hp + 1))
        p4_wo_part(NSUP - 1, range(8))
        K.barrier()

        w2done = K.cnt["pe"]
        w2ev = [(K.cnt["sc"], K.cnt["dve"])]
        w2ld_hold = [None]
        if NC6 > 1:
            for cb_ in range(1, NC6):
                K.dma("qsC", p6_bnd.ap()[:, :, cb_ - 1:cb_],
                      hY[:, 4:8, cb_ * CH6 - 1:cb_ * CH6],
                      deps=[("dve", K.cnt["dve"]), ("pe", K.cnt["pe"]),
                            ("sc", K.cnt["sc"])],
                      allow_slow_non_contiguous=True)
        stats_pe6 = [K.cnt["pe"]]
        ptrB_free = [0]

        def p6_head(c):
            c0 = c * CH6
            dep_pe = ("pe", stats_pe6[0])
            l1 = K.dmaW(c, p6_hb.ap()[:, 0:4, :], hY[:, 0:4, c0:c0 + CH6],
                        deps=[dep_pe, ("sc", K.cnt["sc"])])
            if c0 == 0:
                K.op("gp", "memset", p6_hb.ap()[:, 4:8, 0:1], 0.0, deps=[dep_pe])
                l2 = K.dmaW(c, p6_hb.ap()[:, 4:8, 1:CH6], hY[:, 4:8, 0:CH6 - 1])
                return [l2, l1, ("gp", K.cnt["gp"])]
            l2 = K.dmaW(c, p6_hb.ap()[:, 4:8, 1:CH6],
                        hY[:, 4:8, c0:c0 + CH6 - 1])
            cbd = K.op("dve", "tensor_copy", p6_hb.ap()[:, 4:8, 0:1],
                       p6_bnd.ap()[:, :, c - 1:c],
                       deps=[("qsC", K.cnt["qsC"]), ("sc", K.cnt["sc"]),
                             dep_pe])
            return [l2, l1, ("gp", K.cnt["gp"]), ("dve", cbd)]

        def p6_lnphase(c, dep_h):
            c0 = c * CH6
            lnb = p6_lnd[c % 2]
            sq_pe = stats_pe6[0]
            NS6 = CH6 // 512
            binfo = []
            for sub in range(NS6):
                u0 = sub * 512
                hbs = p6_hb.ap()[:, :, u0:u0 + 512]
                csq = K.op("sc", "activation", p6_sq.ap(), hbs, AF.Square,
                           deps=list(dep_h) + [("pe", sq_pe)])
                if sub == 0:
                    cb = ln_stats(hbs, p6_sq.ap(), dep_h[0], csq=csq)
                    binfo.append((cb, s_B1.ap(), s_B2.ap()))
                else:
                    cb = ln_stats(hbs, p6_sq.ap(), dep_h[0], csq=csq,
                                  bdst=(PTR0f, PTR1f),
                                  bextra=[("dve", ptrB_free[0])])
                    binfo.append((cb, PTR0f, PTR1f))
                sq_pe = K.cnt["pe"]
            for sub in range(NS6):
                u0 = sub * 512
                cb, B1a, B2a = binfo[sub]
                for dt in range(8):
                    fb = dt % 2
                    if u0 == 0 and dt >= 4:
                        if c0 == 0:
                            K.op("gp", "memset", p6_hf[fb].ap()[:, 0:1], 0.0,
                                 deps=[("dve", K.cnt["dve"])])
                        else:
                            K.op("dve", "tensor_copy", p6_hf[fb].ap()[:, 0:1],
                                 p6_bnd.ap()[:, dt - 4, c - 1:c],
                                 deps=[("qsC", K.cnt["qsC"]),
                                       ("qsL" if dt % 2 == 0 else "qsL2",
                                        K.cnt["qsL" if dt % 2 == 0 else "qsL2"])])
                        lf = K.dmaL(dt, p6_hf[fb].ap()[:, 1:512],
                                    hY[:, dt, c0:c0 + 511],
                                    deps=[("dve", K.cnt["dve"])])
                    else:
                        off = c0 + u0 - (1 if dt >= 4 else 0)
                        lf = K.dmaL(dt, p6_hf[fb].ap(), hY[:, dt, off:off + 512],
                                    deps=[("dve", K.cnt["dve"])])
                    K.op("dve", "tensor_mul", p6_t1.ap(), p6_hf[fb].ap(), B1a,
                         deps=[lf, cb, ("gp", K.cnt["gp"])])
                    K.op("dve", "tensor_add", p6_t1.ap(), p6_t1.ap(), B2a)
                    K.op("dve", "tensor_scalar", lnb.ap()[:, dt, u0:u0 + 512],
                         p6_t1.ap(), c_lnG.ap()[:, dt:dt + 1],
                         c_lnB.ap()[:, dt:dt + 1], ALU.mult, ALU.add,
                         deps=[("qsC", K.cnt["qsC"])])
                if sub == NS6 - 1 and NS6 > 1:
                    ptrB_free[0] = K.cnt["dve"]
            stats_pe6[0] = K.cnt["pe"]
            return ("dve", K.cnt["dve"])

        def p6_mms(c, dep_ln):
            nonlocal w2done
            c0 = c * CH6
            lnb = p6_lnd[c % 2]
            if c == 0:
                for k in range(KT):
                    K.dmaW(k, p6_W2.ap()[:, k, :], W2P[l][k],
                           deps=[("pe", w2done)])
                w2ld_hold[0] = [("qgW", K.cnt["qgW"]),
                                ("qgW2", K.cnt["qgW2"])]
            w2_ld = w2ld_hold[0]
            NSUB = CH6 // 512
            aev = [0, 0]
            gev = [0, 0]
            h4d = [0, 0]
            h4k = [[0] * KT, [0] * KT]
            lw1 = {}
            for k in range(min(2, KT)):
                K.dmaW(k, p6_w1a[k % 2].ap(), W1A[l][k], deps=[("pe", w2done)])
                lw1[k] = K.dmaW(k, p6_w1g[k % 2].ap(), W1G[l][k])
            for k in range(KT):
                wb = k % 2
                cmg = 0
                for sub in range(NSUB):
                    u0 = sub * 512
                    psa = PS0a if sub == 0 else PS0b
                    psg = PS1a if sub == 0 else PS1b
                    extra = ([("sc", w2ev[0][0]), ("dve", w2ev[0][1])]
                             if k == 0 else [])
                    cma = K.mmacc(psa, [(p6_w1a[wb].ap()[:, dt, :],
                                         lnb.ap()[:, dt, u0:u0 + 512])
                                        for dt in range(8)],
                                  deps=[dep_ln, lw1[k], ("dve", aev[sub])] + extra)
                    cmg = K.mmacc(psg, [(p6_w1g[wb].ap()[:, dt, :],
                                         lnb.ap()[:, dt, u0:u0 + 512])
                                        for dt in range(8)],
                                  deps=[("sc", gev[sub])] + extra)
                    aev[sub] = K.op("dve", "tensor_scalar", p6_At[sub].ap(), psa,
                                    c_b1a.ap()[:, k:k + 1], None, ALU.add,
                                    deps=[("pe", cma), ("dve", h4d[sub])])
                    gev[sub] = K.op("sc", "activation",
                                    p6_G.ap()[:, k, u0:u0 + 512], psg, AF.Gelu,
                                    bias=c_b1g.ap()[:, k:k + 1],
                                    deps=[("pe", cmg)])
                    h4d[sub] = K.op("dve", "tensor_mul",
                                    p6_G.ap()[:, k, u0:u0 + 512],
                                    p6_At[sub].ap(),
                                    p6_G.ap()[:, k, u0:u0 + 512],
                                    deps=[("sc", gev[sub])])
                    h4k[sub][k] = h4d[sub]
                if k + 2 < KT:
                    K.dmaW(k, p6_w1a[wb].ap(), W1A[l][k + 2],
                           deps=[("pe", cmg)])
                    lw1[k + 2] = K.dmaW(k, p6_w1g[wb].ap(), W1G[l][k + 2])
            bank_sc = [0, 0, gev[0], gev[1]]
            bank_dve = [aev[0], aev[1], 0, 0]
            for sub in range(NSUB):
                u0 = sub * 512
                for dg in range(2):
                    for k in range(KT):
                        for d2 in range(4):
                            ps = (PS0a, PS0b, PS1a, PS1b)[d2]
                            K.wait("pe", w2_ld +
                                   [("pe", K.cnt["pe"]),
                                    ("sc", bank_sc[d2]),
                                    ("dve", max(bank_dve[d2],
                                                h4k[sub][k]))])
                            ins = kb.e["pe"].matmul(
                                ps, p6_W2.ap()[:, k, dg * 512 + d2 * 128:
                                               dg * 512 + (d2 + 1) * 128],
                                p6_G.ap()[:, k, u0:u0 + 512],
                                start=(k == 0), stop=(k == KT - 1))
                            if d2 == 3:
                                kb.cnt["pe"] += 1
                                ins.then_inc(kb.sem["pe"], 1)
                    cmw = K.cnt["pe"]
                    w2done = cmw
                    for d2 in range(4):
                        d = dg * 4 + d2
                        ps = (PS0a, PS0b, PS1a, PS1b)[d2]
                        yb_ = d2 % 2
                        ly = K.dmaL(d2, p6_ys[yb_].ap(),
                                    hY[:, d, c0 + u0:c0 + u0 + 512],
                                    deps=[("dve", K.cnt["dve"])])
                        cse = K.op("sc", "activation", p6_t1.ap(), ps, AF.Identity,
                                   bias=c_b2.ap()[:, d:d + 1],
                                   deps=[("pe", cmw), ("dve", K.cnt["dve"])])
                        bank_sc[d2] = cse
                        if l == 0:
                            cv = K.op("dve", "tensor_add", p6_ys[yb_].ap(),
                                      p6_t1.ap(), p6_ys[yb_].ap(),
                                      deps=[("sc", cse), ly,
                                            ("qsS", K.cnt["qsS"])])
                            K.dma("qsS", hY[:, d, c0 + u0:c0 + u0 + 512],
                                  p6_ys[yb_].ap(), deps=[("dve", cv)])
                        else:
                            lx = K.dmaL(d2, p6_xs[yb_].ap(),
                                        hX1[:, d, c0 + u0:c0 + u0 + 512],
                                        deps=[("dve", K.cnt["dve"])])
                            cv = K.op("dve", "tensor_add", p6_ys[yb_].ap(),
                                      p6_t1.ap(), p6_ys[yb_].ap(),
                                      deps=[("sc", cse), ly,
                                            ("qsS", K.cnt["qsS"])])
                            cv = K.op("dve", "tensor_add", p6_ys[yb_].ap(),
                                      p6_ys[yb_].ap(), p6_xs[yb_].ap(),
                                      deps=[lx])
                            K.dma("qsS", OUT[:, d, c0 + u0:c0 + u0 + 512],
                                  p6_ys[yb_].ap(), deps=[("dve", cv)])
            w2ev[0] = (K.cnt["sc"], K.cnt["dve"])

        dh = p6_head(0)
        cur_ln = p6_lnphase(0, dh)
        if NC6 > 1:
            dh1 = p6_head(1)
        for c in range(NC6):
            nxt_ln = None
            if c + 1 < NC6:
                nxt_ln = p6_lnphase(c + 1, dh1)
            p6_mms(c, cur_ln)
            if c + 2 < NC6:
                dh1 = p6_head(c + 2)
            cur_ln = nxt_ln
        K.barrier()

    K.barrier()
    return kb



def kernel(**inputs):
    x = np.asarray(inputs["L0_attn_x"])
    B, T, _ = x.shape
    shared = host_prep(inputs, T)
    nc = build_nc(T).nc
    in_maps = []
    for b in range(B):
        m = dict(shared)
        m["xT"] = prep_x(x[b])
        in_maps.append(m)
    res = run_bass_kernel_spmd(nc, in_maps, core_ids=list(range(B)))
    out = np.stack([unprep_out(r["OUT"]) for r in res.results], axis=0)
    return out.astype(np.float32)
